# version 02
# baseline (speedup 1.0000x reference)
import contextlib

import numpy as np
import concourse.bass as bass
import concourse.mybir as mybir
from concourse.bass_utils import run_bass_kernel_spmd

F32 = mybir.dt.float32
BF16 = mybir.dt.bfloat16
U8 = mybir.dt.uint8
AF = mybir.ActivationFunctionType
ALU = mybir.AluOpType
AX = mybir.AxisListType

P = 128
D = 2048
KC = D // P
T = 4096
TH = 2048
TQ = 1024
DE = 1408
ME = DE // P
DS = 2816
MS = DS // P
NEXP = 8
HROWS = NEXP * DE + DS
NEK2 = NEXP * ME + MS
EPS = 1e-5

STAGES = 3
DEBUG_DUMP = False
SUBSTAGE = None

NLANES = 8


class _SubstageDone(Exception):
    pass


class DL:

    def __init__(self, sems_):
        self.sems = sems_
        self.cnt = [0] * len(sems_)
        self.i = 0

    def issue(self, eng, **kw):
        L = self.i % len(self.sems)
        self.i += 1
        if self.cnt[L]:
            eng.wait_ge(self.sems[L], 16 * self.cnt[L])
        eng.dma_start(**kw).then_inc(self.sems[L], 16)
        self.cnt[L] += 1
        return (self.sems[L], 16 * self.cnt[L])

    def totals(self):
        return [(s, 16 * c) for s, c in zip(self.sems, self.cnt) if c]


def wtok(eng, toks):
    if toks is None:
        return
    if isinstance(toks, tuple):
        toks = [toks]
    best = {}
    for s, v in toks:
        k = id(s)
        if k not in best or best[k][1] < v:
            best[k] = (s, v)
    for s, v in best.values():
        eng.wait_ge(s, v)


class Ring:

    def __init__(self, slots):
        self.slots = slots
        self.fills = 0
        self.events = []

    def before_fill(self, eng):
        j = self.fills
        self.fills += 1
        if j >= self.slots:
            wtok(eng, self.events[j - self.slots])
        self.events.append([])
        return j % self.slots

    def consumed(self, sem, value, fill=None):
        j = (self.fills - 1) if fill is None else fill
        self.events[j].append((sem, value))


class Arena:
    def __init__(self, ap_u8, nbytes):
        self.ap = ap_u8
        self.n = nbytes
        self.off = 0
        self.marks = []
        self.peak = 0

    def push(self):
        self.marks.append(self.off)

    def pop(self):
        self.off = self.marks.pop()

    def alloc(self, shape, dtype):
        isz = 2 if dtype == BF16 else 4
        nb = shape[-1] * isz
        nb_al = (nb + 63) & ~63
        assert self.off + nb_al <= self.n, \
            f"arena overflow: {self.off}+{nb_al} > {self.n}"
        view = self.ap[:, self.off:self.off + nb].bitcast(dtype)
        self.off += nb_al
        self.peak = max(self.peak, self.off)
        if shape[0] != P:
            view = view[0:shape[0], :]
        return view


def build_nc(stages=STAGES):
    nc = bass.Bass("TRN2", target_bir_lowering=False, debug=False)

    xT = nc.dram_tensor("xT", [D, T], F32, kind="ExternalInput").ap()
    Wk = nc.dram_tensor("Wk", [D, D], F32, kind="ExternalInput").ap()
    Wv = nc.dram_tensor("Wv", [D, D], F32, kind="ExternalInput").ap()
    Wr = nc.dram_tensor("Wr", [D, D], F32, kind="ExternalInput").ap()
    Wo = nc.dram_tensor("Wo", [D, D], F32, kind="ExternalInput").ap()
    lnsc = nc.dram_tensor("lnsc", [D], F32, kind="ExternalInput").ap()
    lnbs = nc.dram_tensor("lnbs", [D], F32, kind="ExternalInput").ap()
    mixk = nc.dram_tensor("mixk", [D], F32, kind="ExternalInput").ap()
    mixv = nc.dram_tensor("mixv", [D], F32, kind="ExternalInput").ap()
    mixr = nc.dram_tensor("mixr", [D], F32, kind="ExternalInput").ap()
    tfirst = nc.dram_tensor("tfirst", [D], F32, kind="ExternalInput").ap()
    tdecay = nc.dram_tensor("tdecay", [D], F32, kind="ExternalInput").ap()
    Wrt = nc.dram_tensor("Wrt", [D, NEXP], F32, kind="ExternalInput").ap()
    Wg = nc.dram_tensor("Wg", [NEXP, D, DE], F32, kind="ExternalInput").ap()
    Wu = nc.dram_tensor("Wu", [NEXP, D, DE], F32, kind="ExternalInput").ap()
    Wdn = nc.dram_tensor("Wdn", [NEXP, DE, D], F32, kind="ExternalInput").ap()
    Wsg = nc.dram_tensor("Wsg", [D, DS], F32, kind="ExternalInput").ap()
    Wsu = nc.dram_tensor("Wsu", [D, DS], F32, kind="ExternalInput").ap()
    Wsd = nc.dram_tensor("Wsd", [DS, D], F32, kind="ExternalInput").ap()
    flag = nc.dram_tensor("flag", [P, 1], F32, kind="ExternalInput").ap()
    outT = nc.dram_tensor("outT", [D, T], F32, kind="ExternalOutput").ap()
    if DEBUG_DUMP:
        dbg_h = nc.dram_tensor("dbg_h", [HROWS, T], BF16,
                               kind="ExternalOutput").ap()
        dbg_cb = nc.dram_tensor("dbg_cb", [NEXP, T], F32,
                                kind="ExternalOutput").ap()
        dbg_y = nc.dram_tensor("dbg_y", [D, T], BF16,
                               kind="ExternalOutput").ap()

    wdT = nc.dram_tensor("wdT", [HROWS, D], BF16).ap()
    h_s = nc.dram_tensor("h_s", [HROWS, T], BF16).ap()
    x1_s = nc.dram_tensor("x1_s", [D, T], F32).ap()
    x2_s = nc.dram_tensor("x2_s", [D, T], F32).ap()
    y2_s = nc.dram_tensor("y2_s", [D, T], F32).ap()
    xn_s = nc.dram_tensor("xn_s", [D, T], F32).ap()
    nl_s = nc.dram_tensor("nl_s", [D, T], F32).ap()
    dl_s = nc.dram_tensor("dl_s", [D, T], F32).ap()
    rs_s = nc.dram_tensor("rs_s", [D, T], F32).ap()
    rw_s = nc.dram_tensor("rw_s", [D, T], BF16).ap()
    cbT_d = nc.dram_tensor("cbT_d", [NEXP, T], F32).ap()
    ccAi = nc.dram_tensor("ccAi", [P, KC], F32).ap()
    ccAo = nc.dram_tensor("ccAo", [2 * P, KC], F32).ap()
    ccBi = nc.dram_tensor("ccBi", [P, 2 * KC], F32).ap()
    ccBo = nc.dram_tensor("ccBo", [2 * P, 2 * KC], F32).ap()

    RG = [[0, 1], [2, 3], [4, 5], [6, 7]]
    C = {s: 0 for s in ("MM", "EV", "AC", "GP")}
    ARENA_B = 196 * 1024

    with contextlib.ExitStack() as ES:
        ones_bf = ES.enter_context(nc.sbuf_tensor("ones_bf", [P, P], BF16))
        identf = ES.enter_context(nc.sbuf_tensor("identf", [P, P], F32))
        vld = ES.enter_context(nc.sbuf_tensor("vld", [KC, 7 * P], F32))
        cols = ES.enter_context(nc.sbuf_tensor("cols", [P, 10 * KC], F32))
        flagc = ES.enter_context(nc.sbuf_tensor("flagc", [P, 1], F32))
        arena_t = ES.enter_context(nc.sbuf_tensor("arena", [P, ARENA_B], U8))
        PSA = ES.enter_context(nc.psum_tensor("PSA", [P, 4096], F32))
        sems = {}
        for s in ("MM", "EV", "AC", "GP", "CCA", "CCB"):
            sems[s] = ES.enter_context(nc.semaphore("s" + s))
        DLS = DL([ES.enter_context(nc.semaphore(f"dS{i}")) for i in range(NLANES)])
        DLA = DL([ES.enter_context(nc.semaphore(f"dA{i}")) for i in range(NLANES)])
        DLG = DL([ES.enter_context(nc.semaphore(f"dG{i}")) for i in range(NLANES)])
        block = ES.enter_context(nc.Block())

        AR = Arena(arena_t[:, :], ARENA_B)

        gcol = cols[:, 0 * KC:1 * KC]
        bcol = cols[:, 1 * KC:2 * KC]
        mkcol = cols[:, 2 * KC:3 * KC]
        mvcol = cols[:, 3 * KC:4 * KC]
        mrcol = cols[:, 4 * KC:5 * KC]
        eucol = cols[:, 5 * KC:6 * KC]
        wcol = cols[:, 6 * KC:7 * KC]
        dcol = cols[:, 7 * KC:8 * KC]
        tmpcol = cols[:, 8 * KC:9 * KC]
        epscol = cols[:, 9 * KC:9 * KC + 1]
        fbv = cols[:, 9 * KC + 1:9 * KC + 2]
        fbg = cols[:, 9 * KC + 2:9 * KC + 3]
        fba = cols[:, 9 * KC + 3:9 * KC + 4]

        def I(res, sem, amt=1):
            res.then_inc(sems[sem], amt)
            C[sem] += amt
            return C[sem]

        def join_all():
            snap = [(sems[s], C[s]) for s in ("MM", "EV", "AC", "GP") if C[s] > 0]
            snap += DLS.totals() + DLA.totals() + DLG.totals()

            def em(eng):
                for s, v in snap:
                    eng.wait_ge(s, v)
            block.sync(em)
            block.scalar(em)
            block.vector(em)
            block.gpsimd(em)
            block.tensor(em)

        tok_vld = []

        def p0_gpsimd(g):
            g.memset(epscol, EPS)
            g.memset(ones_bf[:, :], 1.0 / 2048.0)
            g.memset(identf[:, :], 0.0)
            r = g.affine_select(out=identf[:, :], in_=identf[:, :],
                                compare_op=ALU.not_equal, fill=1.0, base=0,
                                pattern=[[-1, P]], channel_multiplier=1)
            I(r, "GP")
            for i, v in enumerate([lnsc, lnbs, mixk, mixv, mixr, tfirst, tdecay]):
                tok_vld.append(DLG.issue(g, out=vld[:, i * P:(i + 1) * P],
                                         in_=v.rearrange("(a b) -> a b", b=P)))
        block.gpsimd(p0_gpsimd)

        def p0_sync(s):
            DLS.issue(s, out=flagc[:, :], in_=flag[:, :])
        block.sync(p0_sync)

        gp0 = C["GP"]

        def p0_tensor(t):
            wtok(t, tok_vld)
            t.wait_ge(sems["GP"], gp0)
            for i in range(7):
                r = t.transpose(PSA[:, i * KC:(i + 1) * KC],
                                vld[:, i * P:(i + 1) * P], identf[0:KC, 0:KC])
            I(r, "MM")
        block.tensor(p0_tensor)
        mm0 = C["MM"]

        def p0_vector(v):
            v.wait_ge(sems["MM"], mm0)
            for i, dstc in enumerate([gcol, bcol, mkcol, mvcol, mrcol]):
                v.tensor_copy(dstc, PSA[:, i * KC:(i + 1) * KC])
            v.tensor_copy(eucol, PSA[:, 5 * KC:6 * KC])
            r = v.tensor_copy(tmpcol, PSA[:, 6 * KC:7 * KC])
            I(r, "EV")
        block.vector(p0_vector)
        ev0 = C["EV"]

        def p0_scalar(sc):
            sc.wait_ge(sems["EV"], ev0)
            sc.activation(out=eucol, in_=eucol, func=AF.Exp)
            r = sc.activation(out=wcol, in_=tmpcol, func=AF.Exp)
            I(r, "AC")
        block.scalar(p0_scalar)
        ac0 = C["AC"]

        def p0_vector2(v):
            v.wait_ge(sems["AC"], ac0)
            r = v.tensor_scalar(out=wcol, in0=wcol, scalar1=-1.0, scalar2=None,
                                op0=ALU.mult)
            I(r, "EV")
        block.vector(p0_vector2)
        ev1 = C["EV"]

        def p0_scalar2(sc):
            sc.wait_ge(sems["EV"], ev1)
            r = sc.activation(out=dcol, in_=wcol, func=AF.Exp)
            I(r, "AC")
        block.scalar(p0_scalar2)

        join_all()

        def emit_ln_plain(xsrc, ybf3):
            AR.push()
            xf = [AR.alloc([P, TH], F32) for _ in range(2)]
            xbf = [AR.alloc([P, TH], BF16) for _ in range(2)]
            x2bf = [AR.alloc([P, TH], BF16) for _ in range(2)]
            Arep = AR.alloc([P, TH], F32)
            Brep = AR.alloc([P, TH], F32)
            mrep = AR.alloc([P, TH], F32)
            vrep = AR.alloc([P, TH], F32)
            xr = Ring(2)
            cbr_ = Ring(2)
            sbr_ = Ring(2)
            rows_prev = [None]

            for nh in range(2):
                for kc in range(KC):
                    tok = []

                    def ld(s, kc=kc, nh=nh, tok=tok):
                        slot = xr.before_fill(s)
                        tok.append(DLS.issue(s, out=xf[slot][:, :],
                                             in_=xsrc[kc * P:(kc + 1) * P,
                                                      nh * TH:(nh + 1) * TH]))
                    block.sync(ld)
                    xslot = (xr.fills - 1) % 2

                    def cast(v, tok=tok, xslot=xslot):
                        cbr_.before_fill(v)
                        wtok(v, tok)
                        r = v.tensor_copy(xbf[(cbr_.fills - 1) % 2][:, :],
                                          xf[xslot][:, :])
                        I(r, "EV")
                    block.vector(cast)
                    evc = C["EV"]
                    xr.consumed(sems["EV"], evc)

                    def sq(sc, tok=tok, xslot=xslot):
                        sbr_.before_fill(sc)
                        wtok(sc, tok)
                        r = sc.activation(out=x2bf[(sbr_.fills - 1) % 2][:, :],
                                          in_=xf[xslot][:, :], func=AF.Square)
                        I(r, "AC")
                    block.scalar(sq)
                    acc_ = C["AC"]
                    xr.consumed(sems["AC"], acc_)

                    def mm(t, kc=kc, evc=evc, acc_=acc_, wrows=rows_prev[0]):
                        cslot = (cbr_.fills - 1) % 2
                        sslot = (sbr_.fills - 1) % 2
                        t.wait_ge(sems["EV"], evc)
                        t.wait_ge(sems["AC"], acc_)
                        if kc == 0 and wrows is not None:
                            t.wait_ge(sems["EV"], wrows)
                        st = (kc == 0)
                        sp = (kc == KC - 1)
                        for n in range(4):
                            r = t.matmul(PSA[:, n * 512:(n + 1) * 512],
                                         ones_bf[:, :],
                                         xbf[cslot][:, n * 512:(n + 1) * 512],
                                         start=st, stop=sp)
                        I(r, "MM")
                        for n in range(4):
                            r = t.matmul(PSA[:, 2048 + n * 512:2048 + (n + 1) * 512],
                                         ones_bf[:, :],
                                         x2bf[sslot][:, n * 512:(n + 1) * 512],
                                         start=st, stop=sp)
                        I(r, "MM")
                    block.tensor(mm)
                    cbr_.consumed(sems["MM"], C["MM"] - 1)
                    sbr_.consumed(sems["MM"], C["MM"])
                mmS1 = C["MM"]

                def rows(v, mmS1=mmS1):
                    v.wait_ge(sems["MM"], mmS1)
                    v.tensor_copy(mrep[:, :], PSA[:, 0:2048])
                    v.tensor_tensor(out=vrep[:, :], in0=mrep[:, :], in1=mrep[:, :],
                                    op=ALU.mult)
                    v.tensor_tensor(out=vrep[:, :], in0=PSA[:, 2048:4096],
                                    in1=vrep[:, :], op=ALU.subtract)
                    r = v.tensor_copy(fbv, vrep[:, TH - 1:TH])
                    I(r, "EV")
                block.vector(rows)
                evR = C["EV"]
                rows_prev[0] = evR

                def rows_sc(sc, evR=evR):
                    sc.wait_ge(sems["EV"], evR)
                    sc.activation(out=vrep[:, :], in_=vrep[:, :], func=AF.Ln,
                                  bias=epscol)
                    sc.activation(out=Arep[:, :], in_=vrep[:, :], func=AF.Exp,
                                  scale=-0.5)
                    r = sc.activation(out=fba, in_=Arep[:, TH - 1:TH],
                                      func=AF.Copy)
                    I(r, "AC")
                block.scalar(rows_sc)
                acR = C["AC"]

                def rows_v2(v, acR=acR):
                    v.wait_ge(sems["AC"], acR)
                    v.scalar_tensor_tensor(out=Brep[:, :], in0=mrep[:, :],
                                           scalar=-1.0, in1=Arep[:, :],
                                           op0=ALU.mult, op1=ALU.mult)
                    r = v.tensor_copy(fbv, Brep[:, TH - 1:TH])
                    I(r, "EV")
                block.vector(rows_v2)
                evRep = C["EV"]

                for kc in range(KC):
                    tok = []

                    def ld(s, kc=kc, nh=nh, tok=tok):
                        slot = xr.before_fill(s)
                        tok.append(DLS.issue(s, out=xf[slot][:, :],
                                             in_=xsrc[kc * P:(kc + 1) * P,
                                                      nh * TH:(nh + 1) * TH]))
                    block.sync(ld)
                    xslot = (xr.fills - 1) % 2

                    def ap(v, kc=kc, nh=nh, tok=tok, xslot=xslot, evRep=evRep):
                        v.wait_ge(sems["EV"], evRep)
                        wtok(v, tok)
                        v.tensor_tensor(out=mrep[:, :], in0=xf[xslot][:, :],
                                        in1=Arep[:, :], op=ALU.mult)
                        r = v.tensor_tensor(out=ybf3[:, kc, nh * TH:(nh + 1) * TH],
                                            in0=mrep[:, :], in1=Brep[:, :],
                                            op=ALU.add)
                        I(r, "EV")
                    block.vector(ap)
                    xr.consumed(sems["EV"], C["EV"])
            AR.pop()

        def emit_router(ybf3):
            AR.push()
            scT = AR.alloc([NEXP, T], F32)
            S = AR.alloc([P, 32 * NEXP], F32)
            Sm = AR.alloc([P, 32], F32)
            Sr = AR.alloc([P, 32], F32)
            Se = AR.alloc([P, 32 * NEXP], F32)
            wrtb = AR.alloc([P, KC * NEXP], BF16)
            ctT = AR.alloc([NEXP, T], F32)
            tok_w = []

            def r_gp(g):
                tok_w.append(DLG.issue(
                    g, out=wrtb.rearrange("p (kc e) -> p kc e", e=NEXP),
                    in_=Wrt.rearrange("(kc p) e -> p kc e", p=P)))
            block.gpsimd(r_gp)

            def r_mm(t):
                wtok(t, tok_w)
                for kc in range(KC):
                    for n in range(8):
                        r = t.matmul(PSA[0:NEXP, n * 512:(n + 1) * 512],
                                     wrtb[:, kc * NEXP:(kc + 1) * NEXP],
                                     ybf3[:, kc, n * 512:(n + 1) * 512],
                                     start=(kc == 0), stop=(kc == KC - 1))
                I(r, "MM")
            block.tensor(r_mm)
            mm1 = C["MM"]

            def r_ev(v, mm1=mm1):
                v.wait_ge(sems["MM"], mm1)
                r = v.tensor_copy(scT[:, :], PSA[0:NEXP, :])
                I(r, "EV")
            block.vector(r_ev)
            ev1_ = C["EV"]

            def r_tr(t, ev1_=ev1_):
                t.wait_ge(sems["EV"], ev1_)
                for ci in range(32):
                    r = t.transpose(PSA[:, ci * NEXP:(ci + 1) * NEXP],
                                    scT[:, ci * P:(ci + 1) * P],
                                    identf[0:NEXP, 0:NEXP])
                I(r, "MM")
            block.tensor(r_tr)
            mm2 = C["MM"]

            def r_soft(v, mm2=mm2):
                v.wait_ge(sems["MM"], mm2)
                r = v.tensor_copy(S[:, :], PSA[:, 0:32 * NEXP])
                I(r, "EV")
                S3 = S.rearrange("p (a b) -> p a b", b=NEXP)
                Sm3 = Sm.rearrange("p (a b) -> p a b", b=1)
                Se3 = Se.rearrange("p (a b) -> p a b", b=NEXP)
                mb = Sm3.broadcast_to([P, 32, NEXP])
                v.tensor_reduce(out=Sm3, in_=S3, axis=AX.X, op=ALU.max)
                v.tensor_copy(fbv, Sm[:, 31:32])
                v.tensor_tensor(out=Se3, in0=S3, in1=mb, op=ALU.subtract)
                r = v.tensor_copy(fbv, Se[:, 32 * NEXP - 1:32 * NEXP])
                I(r, "EV")
            block.vector(r_soft)
            evS = C["EV"]

            def r_exp(sc, evS=evS):
                sc.wait_ge(sems["EV"], evS)
                sc.activation(out=Se[:, :], in_=Se[:, :], func=AF.Exp)
                r = sc.activation(out=fba, in_=Se[:, 32 * NEXP - 1:32 * NEXP],
                                  func=AF.Copy)
                I(r, "AC")
            block.scalar(r_exp)
            acS = C["AC"]

            def r_top2(v, acS=acS):
                v.wait_ge(sems["AC"], acS)
                S3 = S.rearrange("p (a b) -> p a b", b=NEXP)
                Sm3 = Sm.rearrange("p (a b) -> p a b", b=1)
                Sr3 = Sr.rearrange("p (a b) -> p a b", b=1)
                Se3 = Se.rearrange("p (a b) -> p a b", b=NEXP)
                mb = Sm3.broadcast_to([P, 32, NEXP])
                mbr = Sr3.broadcast_to([P, 32, NEXP])
                v.tensor_reduce(out=Sm3, in_=Se3, axis=AX.X, op=ALU.add)
                v.tensor_copy(fbv, Sm[:, 31:32])
                v.reciprocal(Sr[:, :], Sm[:, :])
                v.tensor_copy(fbv, Sr[:, 31:32])
                v.tensor_tensor(out=Se3, in0=Se3, in1=mbr, op=ALU.mult)
                v.tensor_reduce(out=Sm3, in_=Se3, axis=AX.X, op=ALU.max)
                v.tensor_copy(fbv, Sm[:, 31:32])
                v.tensor_tensor(out=S3, in0=Se3, in1=mb, op=ALU.is_lt)
                v.tensor_tensor(out=S3, in0=Se3, in1=S3, op=ALU.mult)
                v.tensor_reduce(out=Sm3, in_=S3, axis=AX.X, op=ALU.max)
                v.tensor_copy(fbv, Sm[:, 31:32])
                v.tensor_tensor(out=S3, in0=Se3, in1=mb, op=ALU.is_ge)
                v.tensor_tensor(out=Se3, in0=Se3, in1=S3, op=ALU.mult)
                r = v.tensor_copy(fbv, Se[:, 32 * NEXP - 1:32 * NEXP])
                I(r, "EV")
            block.vector(r_top2)
            evT = C["EV"]

            def r_trb(t, evT=evT):
                t.wait_ge(sems["EV"], evT)
                for ci in range(32):
                    r = t.transpose(PSA[0:NEXP, ci * P:(ci + 1) * P],
                                    Se[:, ci * NEXP:(ci + 1) * NEXP], identf[:, :])
                I(r, "MM")
            block.tensor(r_trb)
            mm3 = C["MM"]

            def r_evb(v, mm3=mm3):
                v.wait_ge(sems["MM"], mm3)
                v.tensor_copy(ctT[:, :], PSA[0:NEXP, :])
                r = v.tensor_copy(fbv[0:NEXP, :], ctT[:, T - 1:T])
                I(r, "EV")
            block.vector(r_evb)
            evB = C["EV"]

            def r_st(sc, evB=evB):
                sc.wait_ge(sems["EV"], evB)
                DLA.issue(sc, out=cbT_d[:, :], in_=ctT[:, :])
            block.scalar(r_st)
            AR.pop()

        def emit_gateup(ybf3, inv):
            AR.push()
            gslab = [AR.alloc([P, KC * P], BF16) for _ in range(2)]
            uslab = [AR.alloc([P, KC * P], BF16) for _ in range(2)]
            silu_g = AR.alloc([P, T], F32)
            cb_sb = AR.alloc([P, T], F32)
            hstage = AR.alloc([P, T], BF16)
            wdc = AR.alloc([P, D], BF16) if inv == 0 else None

            gr, ur, hr, wdr = Ring(2), Ring(2), Ring(1), Ring(1)
            cb_prev_gp = [0]
            prev_hev = [None]
            wd_i = [0]

            exlist = [("S", MS)] + [(e, ME) for e in range(NEXP)]

            for ex, MX in exlist:
                tok_cb = []
                if ex != "S":
                    def cb_ld(s, ex=ex, w=cb_prev_gp[0], tok_cb=tok_cb):
                        if w:
                            s.wait_ge(sems["GP"], w)
                        src = cbT_d[ex:ex + 1, :].partition_broadcast(P).squeeze(1)
                        tok_cb.append(DLS.issue(s, out=cb_sb[:, :], in_=src))
                    block.sync(cb_ld)
                ex_last_gp = C["GP"]

                for m in range(MX):
                    tok_sl = []

                    def slabs(g, ex=ex, m=m, tok_sl=tok_sl):
                        if ex == "S":
                            gsrc = Wsg[:, m * P:(m + 1) * P]
                            usrc = Wsu[:, m * P:(m + 1) * P]
                        else:
                            gsrc = Wg[ex, :, m * P:(m + 1) * P]
                            usrc = Wu[ex, :, m * P:(m + 1) * P]
                        sg = gr.before_fill(g)
                        tok_sl.append(DLG.issue(
                            g, out=gslab[sg].rearrange("p (kc m) -> p kc m", m=P),
                            in_=gsrc.rearrange("(kc p) m -> p kc m", p=P)))
                        su = ur.before_fill(g)
                        tok_sl.append(DLG.issue(
                            g, out=uslab[su].rearrange("p (kc m) -> p kc m", m=P),
                            in_=usrc.rearrange("(kc p) m -> p kc m", p=P)))
                        if inv == 0 and wd_i[0] < NEK2:
                            i = wd_i[0]
                            if i < NEXP * ME:
                                e2, k2 = divmod(i, ME)
                                src = Wdn[e2, k2 * P:(k2 + 1) * P, :]
                            else:
                                src = Wsd[(i - NEXP * ME) * P:
                                          (i - NEXP * ME + 1) * P, :]
                            wdr.before_fill(g)
                            tok_sl.append(DLG.issue(g, out=wdc[:, :], in_=src))
                    block.gpsimd(slabs)
                    sl = (gr.fills - 1) % 2

                    if inv == 0 and wd_i[0] < NEK2:
                        i = wd_i[0]
                        wd_i[0] += 1

                        def wd_st(sc, i=i, tok_sl=tok_sl):
                            wtok(sc, tok_sl[-1:])
                            tk = DLA.issue(sc, out=wdT[i * P:(i + 1) * P, :],
                                           in_=wdc[:, :])
                            wdr.consumed(*tk)
                        block.scalar(wd_st)

                    mmg0 = C["MM"]

                    def gate_mm(t, tok_sl=tok_sl, sl=sl, w=prev_hev[0]):
                        wtok(t, tok_sl)
                        if w is not None:
                            t.wait_ge(sems["EV"], w)
                        for k in range(KC):
                            for n in range(8):
                                r = t.matmul(PSA[:, n * 512:(n + 1) * 512],
                                             gslab[sl][:, k * P:(k + 1) * P],
                                             ybf3[:, k, n * 512:(n + 1) * 512],
                                             start=(k == 0), stop=(k == KC - 1))
                                if k == KC - 1:
                                    I(r, "MM")
                    block.tensor(gate_mm)
                    mmg1 = C["MM"]
                    gr.consumed(sems["MM"], mmg1)

                    def gate_silu(sc, mmg0=mmg0):
                        for n in range(8):
                            sc.wait_ge(sems["MM"], mmg0 + n + 1)
                            r = sc.activation(out=silu_g[:, n * 512:(n + 1) * 512],
                                              in_=PSA[:, n * 512:(n + 1) * 512],
                                              func=AF.Silu)
                            I(r, "AC")
                        r = sc.activation(out=fba, in_=silu_g[:, T - 1:T],
                                          func=AF.Copy)
                        I(r, "AC")
                    block.scalar(gate_silu)
                    acg1 = C["AC"]

                    if ex != "S":
                        def gc(g, acg1=acg1, tok_cb=tok_cb):
                            g.wait_ge(sems["AC"], acg1)
                            wtok(g, tok_cb)
                            g.tensor_tensor(out=silu_g[:, :], in0=silu_g[:, :],
                                            in1=cb_sb[:, :], op=ALU.mult)
                            r = g.tensor_copy(fbg, silu_g[:, T - 1:T])
                            I(r, "GP")
                        block.gpsimd(gc)
                        ex_last_gp = C["GP"]
                    gp1 = C["GP"]
                    use_gc = (ex != "S")

                    mmu0 = C["MM"]

                    def up_mm(t, sl=sl, acg1=acg1):
                        for k in range(KC):
                            for n in range(8):
                                if k == 0:
                                    t.wait_ge(sems["AC"], acg1 - 9 + n + 1)
                                r = t.matmul(PSA[:, n * 512:(n + 1) * 512],
                                             uslab[sl][:, k * P:(k + 1) * P],
                                             ybf3[:, k, n * 512:(n + 1) * 512],
                                             start=(k == 0), stop=(k == KC - 1))
                                if k == KC - 1:
                                    I(r, "MM")
                    block.tensor(up_mm)
                    mmu1 = C["MM"]
                    ur.consumed(sems["MM"], mmu1)

                    def h_ev(v, mmu0=mmu0, gp1=gp1, acg1=acg1, use_gc=use_gc):
                        hr.before_fill(v)
                        if use_gc:
                            v.wait_ge(sems["GP"], gp1)
                        else:
                            v.wait_ge(sems["AC"], acg1)
                        for n in range(8):
                            v.wait_ge(sems["MM"], mmu0 + n + 1)
                            r = v.tensor_tensor(out=hstage[:, n * 512:(n + 1) * 512],
                                                in0=silu_g[:, n * 512:(n + 1) * 512],
                                                in1=PSA[:, n * 512:(n + 1) * 512],
                                                op=ALU.mult)
                            I(r, "EV")
                        r = v.tensor_copy(fbv, hstage[:, T - 1:T])
                        I(r, "EV")
                    block.vector(h_ev)
                    evh = C["EV"]
                    prev_hev[0] = evh

                    def h_st(sc, evh=evh, ex=ex, m=m):
                        sc.wait_ge(sems["EV"], evh)
                        row0 = (NEXP * DE + m * P) if ex == "S" else (ex * DE + m * P)
                        tk = DLA.issue(sc, out=h_s[row0:row0 + P, :],
                                       in_=hstage[:, :])
                        hr.consumed(*tk)
                    block.scalar(h_st)

                cb_prev_gp[0] = ex_last_gp
            AR.pop()

        def ek2row(i):
            if i < NEXP * ME:
                e2, k2 = divmod(i, ME)
                return e2 * DE + k2 * P
            return NEXP * DE + (i - NEXP * ME) * P

        def emit_down(xsrc, dst):
            AR.push()
            wdt = [AR.alloc([P, 512], BF16) for _ in range(6)]
            ht = [AR.alloc([P, TQ], BF16) for _ in range(6)]
            xt = [AR.alloc([P, TQ], F32) for _ in range(4)]
            o1 = [AR.alloc([P, TQ], F32) for _ in range(2)]
            wdr = Ring(6)
            hri = Ring(6)
            xri = Ring(4)
            o1r = Ring(2)
            prev_ev = [None, None, None, None]

            for ns in range(4):
                for m2g in range(4):
                    mmd0 = C["MM"]
                    for i in range(NEK2):
                        row = ek2row(i)
                        tok = []

                        def dn_ld(s, row=row, ns=ns, m2g=m2g, tok=tok):
                            sw = wdr.before_fill(s)
                            tok.append(DLS.issue(
                                s, out=wdt[sw][:, :],
                                in_=wdT[row:row + P, m2g * 512:(m2g + 1) * 512]))
                            sh = hri.before_fill(s)
                            tok.append(DLS.issue(
                                s, out=ht[sh][:, :],
                                in_=h_s[row:row + P, ns * TQ:(ns + 1) * TQ]))
                        block.sync(dn_ld)

                        def dn_mm(t, i=i, tok=tok, w=tuple(prev_ev)):
                            wtok(t, tok)
                            sw = (wdr.fills - 1) % 6
                            sh = (hri.fills - 1) % 6
                            for mm2 in range(4):
                                if i == 0 and w[mm2] is not None:
                                    t.wait_ge(sems["EV"], w[mm2])
                                for nsub in range(2):
                                    r = t.matmul(
                                        PSA[:, (mm2 * 2 + nsub) * 512:
                                               (mm2 * 2 + nsub + 1) * 512],
                                        wdt[sw][:, mm2 * P:(mm2 + 1) * P],
                                        ht[sh][:, nsub * 512:(nsub + 1) * 512],
                                        start=(i == 0), stop=(i == NEK2 - 1))
                            I(r, "MM")
                        block.tensor(dn_mm)
                        wdr.consumed(sems["MM"], C["MM"])
                        hri.consumed(sems["MM"], C["MM"])
                    mmd1 = C["MM"]

                    for mm2 in range(4):
                        m2 = m2g * 4 + mm2
                        tokx = []

                        def x_ld(s, m2=m2, ns=ns, tokx=tokx):
                            sx = xri.before_fill(s)
                            tokx.append(DLS.issue(
                                s, out=xt[sx][:, :],
                                in_=xsrc[m2 * P:(m2 + 1) * P,
                                         ns * TQ:(ns + 1) * TQ]))
                        block.sync(x_ld)
                        sx = (xri.fills - 1) % 4

                        def dn_ev(v, mm2=mm2, tokx=tokx, sx=sx, mmd1=mmd1):
                            so = o1r.before_fill(v)
                            v.wait_ge(sems["MM"], mmd1)
                            wtok(v, tokx)
                            bank0 = (mm2 * 2) * 512
                            v.scalar_tensor_tensor(
                                out=o1[so][:, :], in0=PSA[:, bank0:bank0 + TQ],
                                scalar=0.5, in1=xt[sx][:, :],
                                op0=ALU.mult, op1=ALU.add)
                            r = v.tensor_copy(fbv, o1[so][:, TQ - 1:TQ])
                            I(r, "EV")
                        block.vector(dn_ev)
                        evd = C["EV"]
                        prev_ev[mm2] = evd
                        xri.consumed(sems["EV"], evd)

                        def o_st(sc, mm2=mm2, ns=ns, m2g=m2g, evd=evd):
                            sc.wait_ge(sems["EV"], evd)
                            m2 = m2g * 4 + mm2
                            so = (o1r.fills - 1) % 2
                            tk = DLA.issue(sc, out=dst[m2 * P:(m2 + 1) * P,
                                                       ns * TQ:(ns + 1) * TQ],
                                           in_=o1[so][:, :])
                            o1r.consumed(*tk)
                        block.scalar(o_st)
            AR.pop()

        AR.push()
        ybf = AR.alloc([P, KC * T], BF16).rearrange("p (kc t) -> p kc t", t=T)
        emit_ln_plain(xT, ybf)
        join_all()
        emit_router(ybf)
        join_all()
        emit_gateup(ybf, 0)
        if DEBUG_DUMP:
            join_all()

            def dmp(s):
                for kc in range(KC):
                    DLS.issue(s, out=dbg_y[kc * P:(kc + 1) * P, :],
                              in_=ybf[:, kc, :])
                for rr in range(0, HROWS, P):
                    DLS.issue(s, out=dbg_h[rr:rr + P, :],
                              in_=h_s[rr:rr + P, :])
                DLS.issue(s, out=dbg_cb[:, :], in_=cbT_d[:, :])
            block.sync(dmp)
        AR.pop()
        join_all()
        emit_down(xT, x1_s if stages > 1 else outT)
        join_all()

        try:
          if stages >= 2:
            AR.push()
            xnlast = AR.alloc([P, KC], F32)
            prevx = AR.alloc([P, KC], F32)
            finA = AR.alloc([P, KC], F32)
            finB = AR.alloc([P, KC], F32)

            AR.push()
            xf = [AR.alloc([P, TH], F32) for _ in range(3)]
            xbf = [AR.alloc([P, TH], BF16) for _ in range(2)]
            x2bf = [AR.alloc([P, TH], BF16) for _ in range(2)]
            A1 = [AR.alloc([P, TH], F32) for _ in range(2)]
            B1 = [AR.alloc([P, TH], F32) for _ in range(2)]
            A2 = [AR.alloc([P, TH], F32) for _ in range(2)]
            B2 = [AR.alloc([P, TH], F32) for _ in range(2)]
            mrep = AR.alloc([P, TH], F32)
            vrep = AR.alloc([P, TH], F32)
            t1 = AR.alloc([P, TH], F32)
            t2v = [AR.alloc([P, TH], F32) for _ in range(2)]
            y2t = [AR.alloc([P, TH], F32) for _ in range(2)]
            xnt = [AR.alloc([P, TH], F32) for _ in range(2)]
            xr2 = Ring(3)
            cbr2 = Ring(2)
            sbr2 = Ring(2)
            y2r = Ring(2)
            xnr = Ring(2)
            t2r = Ring(2)
            rows2_prev = [None]

            for nh in range(2):
                for kc in range(KC):
                    tok = []

                    def ld(s, kc=kc, nh=nh, tok=tok):
                        slot = xr2.before_fill(s)
                        tok.append(DLS.issue(s, out=xf[slot][:, :],
                                             in_=x1_s[kc * P:(kc + 1) * P,
                                                      nh * TH:(nh + 1) * TH]))
                    block.sync(ld)
                    xslot = (xr2.fills - 1) % 3

                    def cast(v, tok=tok, xslot=xslot):
                        cbr2.before_fill(v)
                        wtok(v, tok)
                        r = v.tensor_copy(xbf[(cbr2.fills - 1) % 2][:, :],
                                          xf[xslot][:, :])
                        I(r, "EV")
                    block.vector(cast)
                    evc = C["EV"]
                    xr2.consumed(sems["EV"], evc)

                    def sq(sc, tok=tok, xslot=xslot):
                        sbr2.before_fill(sc)
                        wtok(sc, tok)
                        r = sc.activation(out=x2bf[(sbr2.fills - 1) % 2][:, :],
                                          in_=xf[xslot][:, :], func=AF.Square)
                        I(r, "AC")
                    block.scalar(sq)
                    acc_ = C["AC"]
                    xr2.consumed(sems["AC"], acc_)

                    def mm(t, kc=kc, evc=evc, acc_=acc_, wrows=rows2_prev[0]):
                        cslot = (cbr2.fills - 1) % 2
                        sslot = (sbr2.fills - 1) % 2
                        t.wait_ge(sems["EV"], evc)
                        t.wait_ge(sems["AC"], acc_)
                        if kc == 0 and wrows is not None:
                            t.wait_ge(sems["EV"], wrows)
                        st = (kc == 0)
                        sp = (kc == KC - 1)
                        for n in range(4):
                            r = t.matmul(PSA[:, n * 512:(n + 1) * 512], ones_bf[:, :],
                                         xbf[cslot][:, n * 512:(n + 1) * 512],
                                         start=st, stop=sp)
                        I(r, "MM")
                        for n in range(4):
                            r = t.matmul(PSA[:, 2048 + n * 512:2048 + (n + 1) * 512],
                                         ones_bf[:, :],
                                         x2bf[sslot][:, n * 512:(n + 1) * 512],
                                         start=st, stop=sp)
                        I(r, "MM")
                    block.tensor(mm)
                    cbr2.consumed(sems["MM"], C["MM"] - 1)
                    sbr2.consumed(sems["MM"], C["MM"])
                mmS1 = C["MM"]

                def rows(v, mmS1=mmS1):
                    v.wait_ge(sems["MM"], mmS1)
                    v.tensor_copy(mrep[:, :], PSA[:, 0:2048])
                    v.tensor_tensor(out=t1[:, :], in0=mrep[:, :], in1=mrep[:, :],
                                    op=ALU.mult)
                    v.tensor_tensor(out=vrep[:, :], in0=PSA[:, 2048:4096],
                                    in1=t1[:, :], op=ALU.subtract)
                    r = v.tensor_copy(fbv, vrep[:, TH - 1:TH])
                    I(r, "EV")
                block.vector(rows)
                evR = C["EV"]
                rows2_prev[0] = evR

                def rows_sc(sc, evR=evR, nh=nh):
                    sc.wait_ge(sems["EV"], evR)
                    sc.activation(out=t1[:, :], in_=vrep[:, :], func=AF.Ln,
                                  bias=epscol)
                    sc.activation(out=A1[nh][:, :], in_=t1[:, :], func=AF.Exp,
                                  scale=-0.5)
                    r = sc.activation(out=fba, in_=A1[nh][:, TH - 1:TH],
                                      func=AF.Copy)
                    I(r, "AC")
                block.scalar(rows_sc)
                acR = C["AC"]

                def rows_v2(v, acR=acR, nh=nh):
                    v.wait_ge(sems["AC"], acR)
                    v.scalar_tensor_tensor(out=B1[nh][:, :], in0=mrep[:, :],
                                           scalar=-1.0, in1=A1[nh][:, :],
                                           op0=ALU.mult, op1=ALU.mult)
                    v.tensor_tensor(out=t1[:, :], in0=A1[nh][:, :], in1=A1[nh][:, :],
                                    op=ALU.mult)
                    v.tensor_tensor(out=t1[:, :], in0=vrep[:, :], in1=t1[:, :],
                                    op=ALU.mult)
                    r = v.tensor_copy(fbv, t1[:, TH - 1:TH])
                    I(r, "EV")
                block.vector(rows_v2)
                evR2 = C["EV"]

                def rows_sc2(sc, evR2=evR2, nh=nh):
                    sc.wait_ge(sems["EV"], evR2)
                    sc.activation(out=t1[:, :], in_=t1[:, :], func=AF.Ln,
                                  bias=epscol)
                    sc.activation(out=vrep[:, :], in_=t1[:, :], func=AF.Exp,
                                  scale=-0.5)
                    r = sc.activation(out=fba, in_=vrep[:, TH - 1:TH],
                                      func=AF.Copy)
                    I(r, "AC")
                block.scalar(rows_sc2)
                acR2 = C["AC"]

                def rows_v3(v, acR2=acR2, nh=nh):
                    v.wait_ge(sems["AC"], acR2)
                    v.tensor_tensor(out=A2[nh][:, :], in0=A1[nh][:, :],
                                    in1=vrep[:, :], op=ALU.mult)
                    v.tensor_tensor(out=B2[nh][:, :], in0=B1[nh][:, :],
                                    in1=vrep[:, :], op=ALU.mult)
                    r = v.tensor_copy(fbv, B2[nh][:, TH - 1:TH])
                    I(r, "EV")
                block.vector(rows_v3)

            for nh in range(2):
                evRep = C["EV"]
                for kc in range(KC):
                    tok = []

                    def ld(s, kc=kc, nh=nh, tok=tok):
                        slot = xr2.before_fill(s)
                        tok.append(DLS.issue(s, out=xf[slot][:, :],
                                             in_=x1_s[kc * P:(kc + 1) * P,
                                                      nh * TH:(nh + 1) * TH]))
                    block.sync(ld)
                    xslot = (xr2.fills - 1) % 3

                    def ap_gp(g, kc=kc, nh=nh, tok=tok, xslot=xslot, evRep=evRep):
                        sy = y2r.before_fill(g)
                        g.wait_ge(sems["EV"], evRep)
                        wtok(g, tok)
                        g.tensor_tensor(out=y2t[sy][:, :], in0=xf[xslot][:, :],
                                        in1=A1[nh][:, :], op=ALU.mult)
                        g.tensor_tensor(out=y2t[sy][:, :], in0=y2t[sy][:, :],
                                        in1=B1[nh][:, :], op=ALU.add)
                        r = g.tensor_copy(fbg, y2t[sy][:, TH - 1:TH])
                        I(r, "GP")
                    block.gpsimd(ap_gp)
                    gpY = C["GP"]

                    def ap_v(v, kc=kc, nh=nh, tok=tok, xslot=xslot, evRep=evRep):
                        sx = xnr.before_fill(v)
                        st2 = t2r.before_fill(v)
                        v.wait_ge(sems["EV"], evRep)
                        wtok(v, tok)
                        v.tensor_tensor(out=t2v[st2][:, :], in0=xf[xslot][:, :],
                                        in1=A2[nh][:, :], op=ALU.mult)
                        v.tensor_tensor(out=t2v[st2][:, :], in0=t2v[st2][:, :],
                                        in1=B2[nh][:, :], op=ALU.add)
                        v.tensor_scalar(out=xnt[sx][:, :], in0=t2v[st2][:, :],
                                        scalar1=gcol[:, kc:kc + 1],
                                        scalar2=bcol[:, kc:kc + 1],
                                        op0=ALU.mult, op1=ALU.add)
                        r = v.tensor_copy(fbv, xnt[sx][:, TH - 1:TH])
                        I(r, "EV")
                        if nh == 1:
                            r = v.tensor_copy(xnlast[:, kc:kc + 1],
                                              xnt[sx][:, TH - 1:TH])
                            I(r, "EV")
                    block.vector(ap_v)
                    evX = C["EV"]
                    xr2.consumed(sems["EV"], evX)
                    t2r.consumed(sems["EV"], evX)

                    def ap_st(sc, kc=kc, nh=nh, gpY=gpY, evX=evX):
                        sc.wait_ge(sems["GP"], gpY)
                        sy = (y2r.fills - 1) % 2
                        tk = DLA.issue(sc, out=y2_s[kc * P:(kc + 1) * P,
                                                    nh * TH:(nh + 1) * TH],
                                       in_=y2t[sy][:, :])
                        y2r.consumed(*tk)
                        sc.wait_ge(sems["EV"], evX)
                        sx = (xnr.fills - 1) % 2
                        tk = DLA.issue(sc, out=xn_s[kc * P:(kc + 1) * P,
                                                    nh * TH:(nh + 1) * TH],
                                       in_=xnt[sx][:, :])
                        xnr.consumed(*tk)
                    block.scalar(ap_st)
            AR.pop()
            join_all()

            if SUBSTAGE == "a":

                def dmp_a(s):
                    for kc in range(KC):
                        DLS.issue(s, out=outT[kc * P:(kc + 1) * P, :],
                                  in_=xn_s[kc * P:(kc + 1) * P, :])
                block.sync(dmp_a)

            if SUBSTAGE == "a":
                raise _SubstageDone()

            def cA_gp(g):
                tk = DLG.issue(g, out=ccAi[:, :], in_=xnlast[:, :])
                wtok(g, tk)
                r = g.collective_compute("AllGather", ALU.bypass,
                                         replica_groups=RG,
                                         ins=[ccAi[:, :].opt()],
                                         outs=[ccAo[:, :].opt()])
                r.then_inc(sems["CCA"], 1)
            block.gpsimd(cA_gp)

            tok_pg = []

            def cA_ld(s):
                s.wait_ge(sems["CCA"], 1)
                tok_pg.append(DLS.issue(s, out=prevx[:, :], in_=ccAo[0:P, :]))
            block.sync(cA_ld)

            def cA_v(v):
                wtok(v, tok_pg)
                r = v.tensor_scalar(out=prevx[:, :], in0=prevx[:, :],
                                    scalar1=flagc[:, 0:1], scalar2=None,
                                    op0=ALU.mult)
                I(r, "EV")
            block.vector(cA_v)
            evPrevx = C["EV"]
            if SUBSTAGE == "ca":

                def dmp_ca(s):
                    s.wait_ge(sems["EV"], evPrevx)
                    for kc in range(KC):
                        DLS.issue(s, out=outT[kc * P:(kc + 1) * P, :],
                                  in_=xn_s[kc * P:(kc + 1) * P, :])
                    DLS.issue(s, out=outT[0:P, T - KC:T], in_=prevx[:, :])
                block.sync(dmp_ca)

            if SUBSTAGE == "ca":
                raise _SubstageDone()

            AR.push()
            kxb = AR.alloc([P, KC * TQ], BF16).rearrange("p (kc t) -> p kc t", t=TQ)
            vxb = AR.alloc([P, KC * TQ], BF16).rearrange("p (kc t) -> p kc t", t=TQ)
            rxb = AR.alloc([P, KC * TQ], BF16).rearrange("p (kc t) -> p kc t", t=TQ)
            xne = [AR.alloc([P, TQ + 1], F32) for _ in range(2)]
            sdif = AR.alloc([P, TQ], F32)
            onesQ = AR.alloc([P, TQ], F32)
            dbc = AR.alloc([P, TQ], F32)
            ekb = AR.alloc([P, TQ], F32)
            ekvb = AR.alloc([P, TQ], F32)
            Abuf = AR.alloc([P, TQ + 1], F32)
            Bbuf = AR.alloc([P, TQ + 1], F32)
            nst = [AR.alloc([P, TQ], F32) for _ in range(2)]
            dstb = [AR.alloc([P, TQ], F32) for _ in range(2)]
            rstb = [AR.alloc([P, TQ], F32) for _ in range(2)]
            trs = AR.alloc([P, TQ], F32)
            kslab = [AR.alloc([P, KC * P], BF16) for _ in range(2)]
            vslab = [AR.alloc([P, KC * P], BF16) for _ in range(2)]
            rslab = [AR.alloc([P, KC * P], BF16) for _ in range(2)]
            xner = Ring(2)
            nring, dring, rring = Ring(2), Ring(2), Ring(2)
            ksr, vsr, rsr = Ring(2), Ring(2), Ring(2)

            def pre2b(g):
                r = g.memset(onesQ[:, :], 1.0)
                I(r, "GP")
            block.gpsimd(pre2b)

            ek_ac = [None]
            ekv_ev = [None]
            rs_ac = [None]

            for q in range(4):
                for kc in range(KC):
                    tok = []

                    def mx_ld(s, q=q, kc=kc, tok=tok):
                        slot = xner.before_fill(s)
                        if q == 0:
                            tok.append(DLS.issue(
                                s, out=xne[slot][:, 1:TQ + 1],
                                in_=xn_s[kc * P:(kc + 1) * P, 0:TQ]))
                            s.wait_ge(sems["EV"], evPrevx)
                            tok.append(DLS.issue(
                                s, out=xne[slot][:, 0:1],
                                in_=prevx[:, kc:kc + 1]))
                        else:
                            tok.append(DLS.issue(
                                s, out=xne[slot][:, :],
                                in_=xn_s[kc * P:(kc + 1) * P,
                                         q * TQ - 1:(q + 1) * TQ]))
                    block.sync(mx_ld)
                    slot = (xner.fills - 1) % 2

                    def mx_v(v, q=q, kc=kc, tok=tok, slot=slot):
                        wtok(v, tok)
                        v.tensor_tensor(out=sdif[:, :], in0=xne[slot][:, 1:TQ + 1],
                                        in1=xne[slot][:, 0:TQ], op=ALU.subtract)
                        v.scalar_tensor_tensor(out=kxb[:, kc, :], in0=sdif[:, :],
                                               scalar=mkcol[:, kc:kc + 1],
                                               in1=xne[slot][:, 0:TQ],
                                               op0=ALU.mult, op1=ALU.add)
                        v.scalar_tensor_tensor(out=vxb[:, kc, :], in0=sdif[:, :],
                                               scalar=mvcol[:, kc:kc + 1],
                                               in1=xne[slot][:, 0:TQ],
                                               op0=ALU.mult, op1=ALU.add)
                        v.scalar_tensor_tensor(out=rxb[:, kc, :], in0=sdif[:, :],
                                               scalar=mrcol[:, kc:kc + 1],
                                               in1=xne[slot][:, 0:TQ],
                                               op0=ALU.mult, op1=ALU.add)
                        r = v.tensor_copy(fbv, rxb[:, kc, TQ - 1:TQ])
                        I(r, "EV")
                    block.vector(mx_v)
                    xner.consumed(sems["EV"], C["EV"])
                mixgp1 = C["GP"]
                mixev1 = C["EV"]

                for m in range(KC):
                    tok_sl = []

                    def at_slab(g, m=m, tok_sl=tok_sl):
                        sk = ksr.before_fill(g)
                        tok_sl.append(DLG.issue(
                            g, out=kslab[sk].rearrange("p (kc m) -> p kc m", m=P),
                            in_=Wk[:, m * P:(m + 1) * P].rearrange(
                                "(kc p) m -> p kc m", p=P)))
                        sv = vsr.before_fill(g)
                        tok_sl.append(DLG.issue(
                            g, out=vslab[sv].rearrange("p (kc m) -> p kc m", m=P),
                            in_=Wv[:, m * P:(m + 1) * P].rearrange(
                                "(kc p) m -> p kc m", p=P)))
                        sr = rsr.before_fill(g)
                        tok_sl.append(DLG.issue(
                            g, out=rslab[sr].rearrange("p (kc m) -> p kc m", m=P),
                            in_=Wr[:, m * P:(m + 1) * P].rearrange(
                                "(kc p) m -> p kc m", p=P)))
                    block.gpsimd(at_slab)
                    sl = (ksr.fills - 1) % 2

                    def at_mm(t, tok_sl=tok_sl, sl=sl, mixgp1=mixgp1,
                              mixev1=mixev1, wek=ek_ac[0], wekv=ekv_ev[0],
                              wrs=rs_ac[0]):
                        wtok(t, tok_sl)
                        t.wait_ge(sems["GP"], mixgp1)
                        t.wait_ge(sems["EV"], mixev1)
                        if wek is not None:
                            t.wait_ge(sems["AC"], wek)
                        if wekv is not None:
                            t.wait_ge(sems["EV"], wekv)
                        for k in range(KC):
                            for n in range(2):
                                r = t.matmul(PSA[:, n * 512:(n + 1) * 512],
                                             kslab[sl][:, k * P:(k + 1) * P],
                                             kxb[:, k, n * 512:(n + 1) * 512],
                                             start=(k == 0), stop=(k == KC - 1))
                        I(r, "MM")
                        for k in range(KC):
                            for n in range(2):
                                r = t.matmul(PSA[:, 1024 + n * 512:1024 + (n + 1) * 512],
                                             vslab[sl][:, k * P:(k + 1) * P],
                                             vxb[:, k, n * 512:(n + 1) * 512],
                                             start=(k == 0), stop=(k == KC - 1))
                        I(r, "MM")
                        if wrs is not None:
                            t.wait_ge(sems["AC"], wrs)
                        for k in range(KC):
                            for n in range(2):
                                r = t.matmul(PSA[:, 2048 + n * 512:2048 + (n + 1) * 512],
                                             rslab[sl][:, k * P:(k + 1) * P],
                                             rxb[:, k, n * 512:(n + 1) * 512],
                                             start=(k == 0), stop=(k == KC - 1))
                        I(r, "MM")
                    block.tensor(at_mm)
                    mmk1 = C["MM"]
                    ksr.consumed(sems["MM"], mmk1)
                    vsr.consumed(sems["MM"], mmk1)
                    rsr.consumed(sems["MM"], mmk1)

                    def at_ek(sc, mmk1=mmk1):
                        sc.wait_ge(sems["MM"], mmk1 - 2)
                        sc.activation(out=ekb[:, :], in_=PSA[:, 0:TQ],
                                      func=AF.Exp)
                        r = sc.activation(out=fba, in_=ekb[:, TQ - 1:TQ],
                                          func=AF.Copy)
                        I(r, "AC")
                        sc.wait_ge(sems["MM"], mmk1)
                        sc.activation(out=trs[:, :], in_=PSA[:, 2048:2048 + TQ],
                                      func=AF.Exp, scale=-1.0)
                        r = sc.activation(out=fba, in_=trs[:, TQ - 1:TQ],
                                          func=AF.Copy)
                        I(r, "AC")
                    block.scalar(at_ek)
                    ek_ac[0] = C["AC"] - 1
                    acEK = C["AC"]

                    def at_scan(v, q=q, m=m, mmk1=mmk1, acEK=acEK):
                        v.wait_ge(sems["AC"], acEK - 1)
                        v.wait_ge(sems["MM"], mmk1 - 1)
                        r = v.tensor_tensor(out=ekvb[:, :], in0=ekb[:, :],
                                            in1=PSA[:, 1024:1024 + TQ], op=ALU.mult)
                        I(r, "EV")
                        v.tensor_scalar(out=dbc[:, :], in0=onesQ[:, :],
                                        scalar1=dcol[:, m:m + 1], scalar2=None,
                                        op0=ALU.mult)
                        if q == 0:
                            v.memset(Abuf[:, 0:1], 0.0)
                            v.memset(Bbuf[:, 0:1], 0.0)
                        else:
                            v.tensor_copy(Abuf[:, 0:1], finA[:, m:m + 1])
                            v.tensor_copy(Bbuf[:, 0:1], finB[:, m:m + 1])
                        v.tensor_tensor_scan(out=Abuf[:, 1:TQ + 1], data0=dbc[:, :],
                                             data1=ekvb[:, :], initial=Abuf[:, 0:1],
                                             op0=ALU.mult, op1=ALU.add)
                        v.tensor_tensor_scan(out=Bbuf[:, 1:TQ + 1], data0=dbc[:, :],
                                             data1=ekb[:, :], initial=Bbuf[:, 0:1],
                                             op0=ALU.mult, op1=ALU.add)
                        v.tensor_copy(finA[:, m:m + 1], Abuf[:, TQ:TQ + 1])
                        v.tensor_copy(finB[:, m:m + 1], Bbuf[:, TQ:TQ + 1])
                        sn = nring.before_fill(v)
                        dring.before_fill(v)
                        v.scalar_tensor_tensor(out=nst[sn][:, :], in0=ekvb[:, :],
                                               scalar=eucol[:, m:m + 1],
                                               in1=Abuf[:, 0:TQ],
                                               op0=ALU.mult, op1=ALU.add)
                        v.scalar_tensor_tensor(out=dstb[sn][:, :], in0=ekb[:, :],
                                               scalar=eucol[:, m:m + 1],
                                               in1=Bbuf[:, 0:TQ],
                                               op0=ALU.mult, op1=ALU.add)
                        v.wait_ge(sems["AC"], acEK)
                        r = v.tensor_scalar(out=trs[:, :], in0=trs[:, :],
                                            scalar1=1.0, scalar2=None, op0=ALU.add)
                        I(r, "EV")
                    block.vector(at_scan)
                    ekv_ev[0] = C["EV"] - 1
                    evND = C["EV"]

                    def at_rs(sc, evND=evND):
                        sr2 = rring.before_fill(sc)
                        sc.wait_ge(sems["EV"], evND)
                        sc.activation(out=rstb[sr2][:, :], in_=trs[:, :], func=AF.Ln)
                        sc.activation(out=rstb[sr2][:, :], in_=rstb[sr2][:, :],
                                      func=AF.Exp, scale=-1.0)
                        r = sc.activation(out=fba, in_=rstb[sr2][:, TQ - 1:TQ],
                                          func=AF.Copy)
                        I(r, "AC")
                    block.scalar(at_rs)
                    rs_ac[0] = C["AC"]
                    acRS = C["AC"]

                    def at_st(sc, q=q, m=m, evND=evND, acRS=acRS):
                        sc.wait_ge(sems["EV"], evND)
                        sn = (nring.fills - 1) % 2
                        tk = DLA.issue(sc, out=nl_s[m * P:(m + 1) * P,
                                                    q * TQ:(q + 1) * TQ],
                                       in_=nst[sn][:, :])
                        nring.consumed(*tk)
                        tk = DLA.issue(sc, out=dl_s[m * P:(m + 1) * P,
                                                    q * TQ:(q + 1) * TQ],
                                       in_=dstb[sn][:, :])
                        dring.consumed(*tk)
                        sc.wait_ge(sems["AC"], acRS)
                        sr2 = (rring.fills - 1) % 2
                        tk = DLA.issue(sc, out=rs_s[m * P:(m + 1) * P,
                                                    q * TQ:(q + 1) * TQ],
                                       in_=rstb[sr2][:, :])
                        rring.consumed(*tk)
                    block.scalar(at_st)
            AR.pop()
            join_all()

            if SUBSTAGE == "b":

                def dmp_b(s):
                    for kc in range(KC):
                        DLS.issue(s, out=outT[kc * P:(kc + 1) * P, :],
                                  in_=nl_s[kc * P:(kc + 1) * P, :])
                block.sync(dmp_b)

            if SUBSTAGE == "b":
                raise _SubstageDone()

            def cB_gp(g):
                t1_ = DLG.issue(g, out=ccBi[:, 0:KC], in_=finA[:, :])
                t2_ = DLG.issue(g, out=ccBi[:, KC:2 * KC], in_=finB[:, :])
                wtok(g, [t1_, t2_])
                r = g.collective_compute("AllGather", ALU.bypass,
                                         replica_groups=RG,
                                         ins=[ccBi[:, :].opt()],
                                         outs=[ccBo[:, :].opt()])
                r.then_inc(sems["CCB"], 1)
            block.gpsimd(cB_gp)

            A0c, B0c = finA, finB
            tok_b0 = []

            def cB_ld(s):
                s.wait_ge(sems["CCB"], 1)
                tok_b0.append(DLS.issue(s, out=A0c[:, :], in_=ccBo[0:P, 0:KC]))
                tok_b0.append(DLS.issue(s, out=B0c[:, :], in_=ccBo[0:P, KC:2 * KC]))
            block.sync(cB_ld)

            def cB_v(v):
                wtok(v, tok_b0)
                v.tensor_scalar(out=A0c[:, :], in0=A0c[:, :],
                                scalar1=flagc[:, 0:1], scalar2=None, op0=ALU.mult)
                r = v.tensor_scalar(out=B0c[:, :], in0=B0c[:, :],
                                    scalar1=flagc[:, 0:1], scalar2=None, op0=ALU.mult)
                I(r, "EV")
            block.vector(cB_v)
            join_all()

            AR.push()
            ramp = AR.alloc([P, T], F32)
            dpow = [AR.alloc([P, TQ], F32) for _ in range(2)]
            rwst = [AR.alloc([P, TQ], BF16) for _ in range(2)]
            nld = [AR.alloc([P, TQ], F32) for _ in range(2)]
            dld = [AR.alloc([P, TQ], F32) for _ in range(2)]
            rld = [AR.alloc([P, TQ], F32) for _ in range(2)]
            rwr = Ring(2)
            nldr, dldr, rldr = Ring(2), Ring(2), Ring(2)

            def c_ramp(g):
                g.iota(ramp[:, :], [[1, T]], channel_multiplier=0,
                       allow_small_or_imprecise_dtypes=True)
                r = g.tensor_copy(fbg, ramp[:, T - 1:T])
                I(r, "GP")
            block.gpsimd(c_ramp)
            gpRamp = C["GP"]
            dp_evs = []

            for q in range(4):
                for m in range(KC):
                    it = q * KC + m
                    ip = it % 2
                    tok = []

                    def pc_ld(s, q=q, m=m, tok=tok):
                        sn = nldr.before_fill(s)
                        tok.append(DLS.issue(s, out=nld[sn][:, :],
                                             in_=nl_s[m * P:(m + 1) * P,
                                                      q * TQ:(q + 1) * TQ]))
                        dldr.before_fill(s)
                        tok.append(DLS.issue(s, out=dld[sn][:, :],
                                             in_=dl_s[m * P:(m + 1) * P,
                                                      q * TQ:(q + 1) * TQ]))
                        rldr.before_fill(s)
                        tok.append(DLS.issue(s, out=rld[sn][:, :],
                                             in_=rs_s[m * P:(m + 1) * P,
                                                      q * TQ:(q + 1) * TQ]))
                    block.sync(pc_ld)
                    sn = (nldr.fills - 1) % 2

                    w_dp = dp_evs[it - 2] if it >= 2 else None

                    def pc_dp(sc, q=q, m=m, ip=ip, gpRamp=gpRamp, w=w_dp):
                        sc.wait_ge(sems["GP"], gpRamp)
                        if w is not None:
                            sc.wait_ge(sems["EV"], w)
                        sc.activation(out=dpow[ip][:, :],
                                      in_=ramp[:, q * TQ:(q + 1) * TQ],
                                      func=AF.Exp, scale=wcol[:, m:m + 1])
                        r = sc.activation(out=fba, in_=dpow[ip][:, TQ - 1:TQ],
                                          func=AF.Copy)
                        I(r, "AC")
                    block.scalar(pc_dp)
                    acDP = C["AC"]

                    def pc_v(v, m=m, ip=ip, tok=tok, acDP=acDP, sn=sn):
                        wtok(v, tok)
                        v.wait_ge(sems["AC"], acDP)
                        v.scalar_tensor_tensor(out=nld[sn][:, :], in0=dpow[ip][:, :],
                                               scalar=A0c[:, m:m + 1],
                                               in1=nld[sn][:, :],
                                               op0=ALU.mult, op1=ALU.add)
                        r = v.scalar_tensor_tensor(out=dld[sn][:, :], in0=dpow[ip][:, :],
                                                   scalar=B0c[:, m:m + 1],
                                                   in1=dld[sn][:, :],
                                                   op0=ALU.mult, op1=ALU.add)
                        I(r, "EV")
                    block.vector(pc_v)
                    dp_evs.append(C["EV"])
                    evC = C["EV"]

                    def pc_sc(sc, evC=evC, sn=sn):
                        sc.wait_ge(sems["EV"], evC)
                        sc.activation(out=dld[sn][:, :], in_=dld[sn][:, :],
                                      func=AF.Ln)
                        sc.activation(out=dld[sn][:, :], in_=dld[sn][:, :],
                                      func=AF.Exp, scale=-1.0)
                        r = sc.activation(out=fba, in_=dld[sn][:, TQ - 1:TQ],
                                          func=AF.Copy)
                        I(r, "AC")
                    block.scalar(pc_sc)
                    acRC = C["AC"]

                    def pc_v2(v, acRC=acRC, sn=sn):
                        sw = rwr.before_fill(v)
                        v.wait_ge(sems["AC"], acRC)
                        v.tensor_tensor(out=nld[sn][:, :], in0=nld[sn][:, :],
                                        in1=dld[sn][:, :], op=ALU.mult)
                        v.tensor_tensor(out=rwst[sw][:, :], in0=nld[sn][:, :],
                                        in1=rld[sn][:, :], op=ALU.mult)
                        r = v.tensor_copy(fbv, rld[sn][:, TQ - 1:TQ])
                        I(r, "EV")
                    block.vector(pc_v2)
                    evRW = C["EV"]
                    nldr.consumed(sems["EV"], evRW)
                    dldr.consumed(sems["EV"], evRW)
                    rldr.consumed(sems["EV"], evRW)

                    def pc_st(sc, q=q, m=m, evRW=evRW):
                        sc.wait_ge(sems["EV"], evRW)
                        sw = (rwr.fills - 1) % 2
                        tk = DLA.issue(sc, out=rw_s[m * P:(m + 1) * P,
                                                    q * TQ:(q + 1) * TQ],
                                       in_=rwst[sw][:, :])
                        rwr.consumed(*tk)
                    block.scalar(pc_st)
            AR.pop()
            AR.pop()
            join_all()

            if SUBSTAGE == "c":

                def dmp_c(g):
                    for kc in range(KC):
                        DLG.issue(g, out=outT[kc * P:(kc + 1) * P, :],
                                  in_=rw_s[kc * P:(kc + 1) * P, :])
                block.gpsimd(dmp_c)

            if SUBSTAGE == "c":
                raise _SubstageDone()

            AR.push()
            rwb = AR.alloc([P, KC * T], BF16).rearrange("p (kc t) -> p kc t", t=T)
            woslab = [AR.alloc([P, KC * P], BF16) for _ in range(2)]
            y2row = [AR.alloc([P, TH], F32) for _ in range(2)]
            x1row = [AR.alloc([P, TH], F32) for _ in range(2)]
            x2st = [AR.alloc([P, TH], F32) for _ in range(2)]
            wor = Ring(2)
            y2rr, x1rr, x2sr = Ring(2), Ring(2), Ring(2)
            tok_rw = []

            def wo_ldrw(s):
                for kc in range(KC):
                    tok_rw.append(DLS.issue(s, out=rwb[:, kc, :],
                                            in_=rw_s[kc * P:(kc + 1) * P, :]))
            block.sync(wo_ldrw)
            wo_prev_ev = [None]

            for m3 in range(KC):
                tok_wo = []

                def wo_slab(g, m3=m3, tok_wo=tok_wo):
                    sw = wor.before_fill(g)
                    tok_wo.append(DLG.issue(
                        g, out=woslab[sw].rearrange("p (kc m) -> p kc m", m=P),
                        in_=Wo[:, m3 * P:(m3 + 1) * P].rearrange(
                            "(kc p) m -> p kc m", p=P)))
                block.gpsimd(wo_slab)

                mmw0 = C["MM"]

                def wo_mm(t, tok_wo=tok_wo, w=wo_prev_ev[0]):
                    wtok(t, tok_wo + tok_rw)
                    if w is not None:
                        t.wait_ge(sems["EV"], w)
                    sw = (wor.fills - 1) % 2
                    for k3 in range(KC):
                        for n in range(8):
                            r = t.matmul(PSA[:, n * 512:(n + 1) * 512],
                                         woslab[sw][:, k3 * P:(k3 + 1) * P],
                                         rwb[:, k3, n * 512:(n + 1) * 512],
                                         start=(k3 == 0), stop=(k3 == KC - 1))
                            if k3 == KC - 1:
                                I(r, "MM")
                block.tensor(wo_mm)
                mmw1 = C["MM"]
                wor.consumed(sems["MM"], mmw1)

                for half in range(2):
                    tok_rows = []

                    def wo_rows(s, m3=m3, half=half, tok_rows=tok_rows):
                        sy = y2rr.before_fill(s)
                        tok_rows.append(DLS.issue(
                            s, out=y2row[sy][:, :],
                            in_=y2_s[m3 * P:(m3 + 1) * P,
                                     half * TH:(half + 1) * TH]))
                        x1rr.before_fill(s)
                        tok_rows.append(DLS.issue(
                            s, out=x1row[sy][:, :],
                            in_=x1_s[m3 * P:(m3 + 1) * P,
                                     half * TH:(half + 1) * TH]))
                    block.sync(wo_rows)

                    def wo_ev(v, mmw0=mmw0, half=half, tok_rows=tok_rows):
                        sx2 = x2sr.before_fill(v)
                        wtok(v, tok_rows)
                        sy = (y2rr.fills - 1) % 2
                        for n4 in range(4):
                            n = half * 4 + n4
                            v.wait_ge(sems["MM"], mmw0 + n + 1)
                            slp = slice(n * 512, (n + 1) * 512)
                            sll = slice(n4 * 512, (n4 + 1) * 512)
                            v.tensor_tensor(out=x2st[sx2][:, sll], in0=PSA[:, slp],
                                            in1=y2row[sy][:, sll], op=ALU.add)
                            v.tensor_tensor(out=x2st[sx2][:, sll],
                                            in0=x2st[sx2][:, sll],
                                            in1=x1row[sy][:, sll], op=ALU.add)
                        r = v.tensor_copy(fbv, x2st[sx2][:, TH - 1:TH])
                        I(r, "EV")
                    block.vector(wo_ev)
                    y2rr.consumed(sems["EV"], C["EV"])
                    x1rr.consumed(sems["EV"], C["EV"])
                    evWo = C["EV"]

                    def wo_st(sc, m3=m3, half=half, evWo=evWo):
                        sc.wait_ge(sems["EV"], evWo)
                        sx2 = (x2sr.fills - 1) % 2
                        dst2 = x2_s if stages > 2 else outT
                        tk = DLA.issue(sc, out=dst2[m3 * P:(m3 + 1) * P,
                                                    half * TH:(half + 1) * TH],
                                       in_=x2st[sx2][:, :])
                        x2sr.consumed(*tk)
                    block.scalar(wo_st)
                wo_prev_ev[0] = C["EV"]
            AR.pop()
            join_all()

          if stages >= 3:
            AR.push()
            ybf3b = AR.alloc([P, KC * T], BF16).rearrange("p (kc t) -> p kc t", t=T)
            emit_ln_plain(x2_s, ybf3b)
            join_all()
            emit_router(ybf3b)
            join_all()
            emit_gateup(ybf3b, 1)
            AR.pop()
            join_all()
            emit_down(x2_s, outT)
            join_all()
        except _SubstageDone:
            join_all()

        def fin(sc):
            for s, v in DLA.totals() + DLS.totals() + DLG.totals():
                sc.wait_ge(s, v)
        block.scalar(fin)

    return nc


_NC_CACHE = {}


def kernel(L0_attn_x, L0_attn_ln_scale, L0_attn_ln_bias, L0_attn_mix_k,
           L0_attn_mix_v, L0_attn_mix_r, L0_attn_time_first, L0_attn_time_decay,
           L0_attn_Wk, L0_attn_Wv, L0_attn_Wr, L0_attn_Wo, L0_ffn_W_router,
           L0_ffn_We_gate, L0_ffn_We_up, L0_ffn_We_down, L0_ffn_Ws_gate,
           L0_ffn_Ws_up, L0_ffn_Ws_down):
    x = np.asarray(L0_attn_x, dtype=np.float32)
    B, S, _ = x.shape
    nco = 8

    key = (STAGES, SUBSTAGE, DEBUG_DUMP)
    if key not in _NC_CACHE:
        _NC_CACHE[key] = build_nc(STAGES)
    nc = _NC_CACHE[key]

    common = {
        "Wk": np.ascontiguousarray(L0_attn_Wk, dtype=np.float32),
        "Wv": np.ascontiguousarray(L0_attn_Wv, dtype=np.float32),
        "Wr": np.ascontiguousarray(L0_attn_Wr, dtype=np.float32),
        "Wo": np.ascontiguousarray(L0_attn_Wo, dtype=np.float32),
        "lnsc": np.ascontiguousarray(L0_attn_ln_scale, dtype=np.float32),
        "lnbs": np.ascontiguousarray(L0_attn_ln_bias, dtype=np.float32),
        "mixk": np.ascontiguousarray(L0_attn_mix_k, dtype=np.float32),
        "mixv": np.ascontiguousarray(L0_attn_mix_v, dtype=np.float32),
        "mixr": np.ascontiguousarray(L0_attn_mix_r, dtype=np.float32),
        "tfirst": np.ascontiguousarray(L0_attn_time_first, dtype=np.float32),
        "tdecay": np.ascontiguousarray(L0_attn_time_decay, dtype=np.float32),
        "Wrt": np.ascontiguousarray(L0_ffn_W_router, dtype=np.float32),
        "Wg": np.ascontiguousarray(L0_ffn_We_gate, dtype=np.float32),
        "Wu": np.ascontiguousarray(L0_ffn_We_up, dtype=np.float32),
        "Wdn": np.ascontiguousarray(L0_ffn_We_down, dtype=np.float32),
        "Wsg": np.ascontiguousarray(L0_ffn_Ws_gate, dtype=np.float32),
        "Wsu": np.ascontiguousarray(L0_ffn_Ws_up, dtype=np.float32),
        "Wsd": np.ascontiguousarray(L0_ffn_Ws_down, dtype=np.float32),
    }
    in_maps = []
    for c in range(nco):
        b, half = c // 2, c % 2
        xs = np.ascontiguousarray(x[b, half * T:(half + 1) * T, :].T)
        fl = np.full((P, 1), 1.0 if half == 1 else 0.0, dtype=np.float32)
        m = dict(common)
        m["xT"] = xs
        m["flag"] = fl
        in_maps.append(m)

    res = run_bass_kernel_spmd(nc, in_maps, core_ids=list(range(nco)))

    out = np.empty((B, S, D), dtype=np.float32)
    for c in range(nco):
        b, half = c // 2, c % 2
        out[b, half * T:(half + 1) * T, :] = res.results[c]["outT"].T
    return out


# version 05
# speedup vs baseline: 1.0862x; 1.0862x over previous
import contextlib

import numpy as np
import concourse.bass as bass
import concourse.mybir as mybir
from concourse.bass_utils import run_bass_kernel_spmd

F32 = mybir.dt.float32
BF16 = mybir.dt.bfloat16
U8 = mybir.dt.uint8
AF = mybir.ActivationFunctionType
ALU = mybir.AluOpType
AX = mybir.AxisListType

P = 128
D = 2048
KC = D // P
T = 4096
TH = 2048
TQ = 1024
DE = 1408
ME = DE // P
DS = 2816
MS = DS // P
NEXP = 8
HROWS = NEXP * DE + DS
NEK2 = NEXP * ME + MS
EPS = 1e-5

STAGES = 3
DEBUG_DUMP = False
SUBSTAGE = None

NLANES = 8


class _SubstageDone(Exception):
    pass


class DL:

    def __init__(self, sems_):
        self.sems = sems_
        self.cnt = [0] * len(sems_)
        self.i = 0

    def issue(self, eng, **kw):
        L = self.i % len(self.sems)
        self.i += 1
        if self.cnt[L]:
            eng.wait_ge(self.sems[L], 16 * self.cnt[L])
        eng.dma_start(**kw).then_inc(self.sems[L], 16)
        self.cnt[L] += 1
        return (self.sems[L], 16 * self.cnt[L])

    def totals(self):
        return [(s, 16 * c) for s, c in zip(self.sems, self.cnt) if c]


def wtok(eng, toks):
    if toks is None:
        return
    if isinstance(toks, tuple):
        toks = [toks]
    best = {}
    for s, v in toks:
        k = id(s)
        if k not in best or best[k][1] < v:
            best[k] = (s, v)
    for s, v in best.values():
        eng.wait_ge(s, v)


class Ring:

    def __init__(self, slots):
        self.slots = slots
        self.fills = 0
        self.events = []

    def before_fill(self, eng):
        j = self.fills
        self.fills += 1
        if j >= self.slots:
            wtok(eng, self.events[j - self.slots])
        self.events.append([])
        return j % self.slots

    def consumed(self, sem, value, fill=None):
        j = (self.fills - 1) if fill is None else fill
        self.events[j].append((sem, value))


class Arena:
    def __init__(self, ap_u8, nbytes):
        self.ap = ap_u8
        self.n = nbytes
        self.off = 0
        self.marks = []
        self.peak = 0

    def push(self):
        self.marks.append(self.off)

    def pop(self):
        self.off = self.marks.pop()

    def alloc(self, shape, dtype):
        isz = 2 if dtype == BF16 else 4
        nb = shape[-1] * isz
        nb_al = (nb + 63) & ~63
        assert self.off + nb_al <= self.n, \
            f"arena overflow: {self.off}+{nb_al} > {self.n}"
        view = self.ap[:, self.off:self.off + nb].bitcast(dtype)
        self.off += nb_al
        self.peak = max(self.peak, self.off)
        if shape[0] != P:
            view = view[0:shape[0], :]
        return view


def build_nc(stages=STAGES):
    nc = bass.Bass("TRN2", target_bir_lowering=False, debug=False)

    xT = nc.dram_tensor("xT", [D, T], F32, kind="ExternalInput").ap()
    Wk = nc.dram_tensor("Wk", [D, D], F32, kind="ExternalInput").ap()
    Wv = nc.dram_tensor("Wv", [D, D], F32, kind="ExternalInput").ap()
    Wr = nc.dram_tensor("Wr", [D, D], F32, kind="ExternalInput").ap()
    Wo = nc.dram_tensor("Wo", [D, D], F32, kind="ExternalInput").ap()
    lnsc = nc.dram_tensor("lnsc", [D], F32, kind="ExternalInput").ap()
    lnbs = nc.dram_tensor("lnbs", [D], F32, kind="ExternalInput").ap()
    mixk = nc.dram_tensor("mixk", [D], F32, kind="ExternalInput").ap()
    mixv = nc.dram_tensor("mixv", [D], F32, kind="ExternalInput").ap()
    mixr = nc.dram_tensor("mixr", [D], F32, kind="ExternalInput").ap()
    tfirst = nc.dram_tensor("tfirst", [D], F32, kind="ExternalInput").ap()
    tdecay = nc.dram_tensor("tdecay", [D], F32, kind="ExternalInput").ap()
    Wrt = nc.dram_tensor("Wrt", [D, NEXP], F32, kind="ExternalInput").ap()
    Wg = nc.dram_tensor("Wg", [NEXP, D, DE], F32, kind="ExternalInput").ap()
    Wu = nc.dram_tensor("Wu", [NEXP, D, DE], F32, kind="ExternalInput").ap()
    Wdn = nc.dram_tensor("Wdn", [NEXP, DE, D], F32, kind="ExternalInput").ap()
    Wsg = nc.dram_tensor("Wsg", [D, DS], F32, kind="ExternalInput").ap()
    Wsu = nc.dram_tensor("Wsu", [D, DS], F32, kind="ExternalInput").ap()
    Wsd = nc.dram_tensor("Wsd", [DS, D], F32, kind="ExternalInput").ap()
    flag = nc.dram_tensor("flag", [P, 1], F32, kind="ExternalInput").ap()
    outT = nc.dram_tensor("outT", [D, T], F32, kind="ExternalOutput").ap()
    if DEBUG_DUMP:
        dbg_h = nc.dram_tensor("dbg_h", [HROWS, T], BF16,
                               kind="ExternalOutput").ap()
        dbg_cb = nc.dram_tensor("dbg_cb", [NEXP, T], F32,
                                kind="ExternalOutput").ap()
        dbg_y = nc.dram_tensor("dbg_y", [D, T], BF16,
                               kind="ExternalOutput").ap()

    wdT = nc.dram_tensor("wdT", [HROWS, D], BF16).ap()
    h_s = nc.dram_tensor("h_s", [HROWS, T], BF16).ap()
    x1_s = nc.dram_tensor("x1_s", [D, T], F32).ap()
    x2_s = nc.dram_tensor("x2_s", [D, T], F32).ap()
    y2_s = nc.dram_tensor("y2_s", [D, T], F32).ap()
    xn_s = nc.dram_tensor("xn_s", [D, T], F32).ap()
    nl_s = nc.dram_tensor("nl_s", [D, T], F32).ap()
    dl_s = nc.dram_tensor("dl_s", [D, T], F32).ap()
    rs_s = nc.dram_tensor("rs_s", [D, T], F32).ap()
    rw_s = nc.dram_tensor("rw_s", [D, T], BF16).ap()
    cbT_d = nc.dram_tensor("cbT_d", [NEXP, T], F32).ap()
    ab_s = nc.dram_tensor("ab_s", [4, TH], F32).ap()
    ccAi = nc.dram_tensor("ccAi", [P, KC], F32).ap()
    ccAo = nc.dram_tensor("ccAo", [2 * P, KC], F32).ap()
    ccBi = nc.dram_tensor("ccBi", [P, 2 * KC], F32).ap()
    ccBo = nc.dram_tensor("ccBo", [2 * P, 2 * KC], F32).ap()

    RG = [[0, 1], [2, 3], [4, 5], [6, 7]]
    C = {s: 0 for s in ("MM", "EV", "AC", "GP")}
    ARENA_B = 196 * 1024

    with contextlib.ExitStack() as ES:
        ones_bf = ES.enter_context(nc.sbuf_tensor("ones_bf", [P, P], BF16))
        identf = ES.enter_context(nc.sbuf_tensor("identf", [P, P], F32))
        vld = ES.enter_context(nc.sbuf_tensor("vld", [KC, 7 * P], F32))
        cols = ES.enter_context(nc.sbuf_tensor("cols", [P, 10 * KC], F32))
        flagc = ES.enter_context(nc.sbuf_tensor("flagc", [P, 1], F32))
        arena_t = ES.enter_context(nc.sbuf_tensor("arena", [P, ARENA_B], U8))
        PSA = ES.enter_context(nc.psum_tensor("PSA", [P, 4096], F32))
        sems = {}
        for s in ("MM", "EV", "AC", "GP", "CCA", "CCB"):
            sems[s] = ES.enter_context(nc.semaphore("s" + s))
        DLS = DL([ES.enter_context(nc.semaphore(f"dS{i}")) for i in range(NLANES)])
        DLA = DL([ES.enter_context(nc.semaphore(f"dA{i}")) for i in range(NLANES)])
        DLG = DL([ES.enter_context(nc.semaphore(f"dG{i}")) for i in range(NLANES)])
        block = ES.enter_context(nc.Block())

        AR = Arena(arena_t[:, :], ARENA_B)

        gcol = cols[:, 0 * KC:1 * KC]
        bcol = cols[:, 1 * KC:2 * KC]
        mkcol = cols[:, 2 * KC:3 * KC]
        mvcol = cols[:, 3 * KC:4 * KC]
        mrcol = cols[:, 4 * KC:5 * KC]
        eucol = cols[:, 5 * KC:6 * KC]
        wcol = cols[:, 6 * KC:7 * KC]
        dcol = cols[:, 7 * KC:8 * KC]
        tmpcol = cols[:, 8 * KC:9 * KC]
        epscol = cols[:, 9 * KC:9 * KC + 1]
        fbv = cols[:, 9 * KC + 1:9 * KC + 2]
        fbg = cols[:, 9 * KC + 2:9 * KC + 3]
        fba = cols[:, 9 * KC + 3:9 * KC + 4]

        def I(res, sem, amt=1):
            res.then_inc(sems[sem], amt)
            C[sem] += amt
            return C[sem]

        def join_all():
            snap = [(sems[s], C[s]) for s in ("MM", "EV", "AC", "GP") if C[s] > 0]
            snap += DLS.totals() + DLA.totals() + DLG.totals()

            def em(eng):
                for s, v in snap:
                    eng.wait_ge(s, v)
            block.sync(em)
            block.scalar(em)
            block.vector(em)
            block.gpsimd(em)
            block.tensor(em)

        tok_vld = []

        def p0_gpsimd(g):
            g.memset(epscol, EPS)
            g.memset(ones_bf[:, :], 1.0 / 2048.0)
            g.memset(identf[:, :], 0.0)
            r = g.affine_select(out=identf[:, :], in_=identf[:, :],
                                compare_op=ALU.not_equal, fill=1.0, base=0,
                                pattern=[[-1, P]], channel_multiplier=1)
            I(r, "GP")
            for i, v in enumerate([lnsc, lnbs, mixk, mixv, mixr, tfirst, tdecay]):
                tok_vld.append(DLG.issue(g, out=vld[:, i * P:(i + 1) * P],
                                         in_=v.rearrange("(a b) -> a b", b=P)))
        block.gpsimd(p0_gpsimd)

        def p0_sync(s):
            DLS.issue(s, out=flagc[:, :], in_=flag[:, :])
        block.sync(p0_sync)

        gp0 = C["GP"]

        def p0_tensor(t):
            wtok(t, tok_vld)
            t.wait_ge(sems["GP"], gp0)
            for i in range(7):
                r = t.transpose(PSA[:, i * KC:(i + 1) * KC],
                                vld[:, i * P:(i + 1) * P], identf[0:KC, 0:KC])
            I(r, "MM")
        block.tensor(p0_tensor)
        mm0 = C["MM"]

        def p0_vector(v):
            v.wait_ge(sems["MM"], mm0)
            for i, dstc in enumerate([gcol, bcol, mkcol, mvcol, mrcol]):
                v.tensor_copy(dstc, PSA[:, i * KC:(i + 1) * KC])
            v.tensor_copy(eucol, PSA[:, 5 * KC:6 * KC])
            r = v.tensor_copy(tmpcol, PSA[:, 6 * KC:7 * KC])
            I(r, "EV")
        block.vector(p0_vector)
        ev0 = C["EV"]

        def p0_scalar(sc):
            sc.wait_ge(sems["EV"], ev0)
            sc.activation(out=eucol, in_=eucol, func=AF.Exp)
            r = sc.activation(out=wcol, in_=tmpcol, func=AF.Exp)
            I(r, "AC")
        block.scalar(p0_scalar)
        ac0 = C["AC"]

        def p0_vector2(v):
            v.wait_ge(sems["AC"], ac0)
            r = v.tensor_scalar(out=wcol, in0=wcol, scalar1=-1.0, scalar2=None,
                                op0=ALU.mult)
            I(r, "EV")
        block.vector(p0_vector2)
        ev1 = C["EV"]

        def p0_scalar2(sc):
            sc.wait_ge(sems["EV"], ev1)
            r = sc.activation(out=dcol, in_=wcol, func=AF.Exp)
            I(r, "AC")
        block.scalar(p0_scalar2)

        join_all()

        def emit_ln_plain(xsrc, ybf3):
            AR.push()
            xf = [AR.alloc([P, TH], F32) for _ in range(2)]
            xbf = [AR.alloc([P, TH], BF16) for _ in range(2)]
            x2bf = [AR.alloc([P, TH], BF16) for _ in range(2)]
            Arep = AR.alloc([P, TH], F32)
            Brep = AR.alloc([P, TH], F32)
            mrep = AR.alloc([P, TH], F32)
            vrep = AR.alloc([P, TH], F32)
            xr = Ring(2)
            cbr_ = Ring(2)
            sbr_ = Ring(2)
            rows_prev = [None]

            for nh in range(2):
                for kc in range(KC):
                    tok = []

                    def ld(s, kc=kc, nh=nh, tok=tok):
                        slot = xr.before_fill(s)
                        tok.append(DLS.issue(s, out=xf[slot][:, :],
                                             in_=xsrc[kc * P:(kc + 1) * P,
                                                      nh * TH:(nh + 1) * TH]))
                    block.sync(ld)
                    xslot = (xr.fills - 1) % 2

                    def cast(v, tok=tok, xslot=xslot):
                        cbr_.before_fill(v)
                        wtok(v, tok)
                        r = v.tensor_copy(xbf[(cbr_.fills - 1) % 2][:, :],
                                          xf[xslot][:, :])
                        I(r, "EV")
                    block.vector(cast)
                    evc = C["EV"]
                    xr.consumed(sems["EV"], evc)

                    def sq(sc, tok=tok, xslot=xslot):
                        sbr_.before_fill(sc)
                        wtok(sc, tok)
                        r = sc.activation(out=x2bf[(sbr_.fills - 1) % 2][:, :],
                                          in_=xf[xslot][:, :], func=AF.Square)
                        I(r, "AC")
                    block.scalar(sq)
                    acc_ = C["AC"]
                    xr.consumed(sems["AC"], acc_)

                    def mm(t, kc=kc, evc=evc, acc_=acc_, wrows=rows_prev[0]):
                        cslot = (cbr_.fills - 1) % 2
                        sslot = (sbr_.fills - 1) % 2
                        t.wait_ge(sems["EV"], evc)
                        t.wait_ge(sems["AC"], acc_)
                        if kc == 0 and wrows is not None:
                            t.wait_ge(sems["EV"], wrows)
                        st = (kc == 0)
                        sp = (kc == KC - 1)
                        for n in range(4):
                            r = t.matmul(PSA[:, n * 512:(n + 1) * 512],
                                         ones_bf[:, :],
                                         xbf[cslot][:, n * 512:(n + 1) * 512],
                                         start=st, stop=sp)
                        I(r, "MM")
                        for n in range(4):
                            r = t.matmul(PSA[:, 2048 + n * 512:2048 + (n + 1) * 512],
                                         ones_bf[:, :],
                                         x2bf[sslot][:, n * 512:(n + 1) * 512],
                                         start=st, stop=sp)
                        I(r, "MM")
                    block.tensor(mm)
                    cbr_.consumed(sems["MM"], C["MM"] - 1)
                    sbr_.consumed(sems["MM"], C["MM"])
                mmS1 = C["MM"]

                def rows(v, mmS1=mmS1):
                    v.wait_ge(sems["MM"], mmS1)
                    v.tensor_copy(mrep[:, :], PSA[:, 0:2048])
                    v.tensor_tensor(out=vrep[:, :], in0=mrep[:, :], in1=mrep[:, :],
                                    op=ALU.mult)
                    v.tensor_tensor(out=vrep[:, :], in0=PSA[:, 2048:4096],
                                    in1=vrep[:, :], op=ALU.subtract)
                    r = v.tensor_copy(fbv, vrep[:, TH - 1:TH])
                    I(r, "EV")
                block.vector(rows)
                evR = C["EV"]
                rows_prev[0] = evR

                def rows_sc(sc, evR=evR):
                    sc.wait_ge(sems["EV"], evR)
                    sc.activation(out=vrep[:, :], in_=vrep[:, :], func=AF.Ln,
                                  bias=epscol)
                    sc.activation(out=Arep[:, :], in_=vrep[:, :], func=AF.Exp,
                                  scale=-0.5)
                    r = sc.activation(out=fba, in_=Arep[:, TH - 1:TH],
                                      func=AF.Copy)
                    I(r, "AC")
                block.scalar(rows_sc)
                acR = C["AC"]

                def rows_v2(v, acR=acR):
                    v.wait_ge(sems["AC"], acR)
                    v.scalar_tensor_tensor(out=Brep[:, :], in0=mrep[:, :],
                                           scalar=-1.0, in1=Arep[:, :],
                                           op0=ALU.mult, op1=ALU.mult)
                    r = v.tensor_copy(fbv, Brep[:, TH - 1:TH])
                    I(r, "EV")
                block.vector(rows_v2)
                evRep = C["EV"]

                for kc in range(KC):
                    tok = []

                    def ld(s, kc=kc, nh=nh, tok=tok):
                        slot = xr.before_fill(s)
                        tok.append(DLS.issue(s, out=xf[slot][:, :],
                                             in_=xsrc[kc * P:(kc + 1) * P,
                                                      nh * TH:(nh + 1) * TH]))
                    block.sync(ld)
                    xslot = (xr.fills - 1) % 2

                    def ap(v, kc=kc, nh=nh, tok=tok, xslot=xslot, evRep=evRep):
                        v.wait_ge(sems["EV"], evRep)
                        wtok(v, tok)
                        v.tensor_tensor(out=mrep[:, :], in0=xf[xslot][:, :],
                                        in1=Arep[:, :], op=ALU.mult)
                        r = v.tensor_tensor(out=ybf3[:, kc, nh * TH:(nh + 1) * TH],
                                            in0=mrep[:, :], in1=Brep[:, :],
                                            op=ALU.add)
                        I(r, "EV")
                    block.vector(ap)
                    xr.consumed(sems["EV"], C["EV"])
            AR.pop()

        def emit_router(ybf3):
            AR.push()
            scT = AR.alloc([NEXP, T], F32)
            S = AR.alloc([P, 32 * NEXP], F32)
            Sm = AR.alloc([P, 32], F32)
            Sr = AR.alloc([P, 32], F32)
            Se = AR.alloc([P, 32 * NEXP], F32)
            wrtb = AR.alloc([P, KC * NEXP], BF16)
            ctT = AR.alloc([NEXP, T], F32)
            tok_w = []

            def r_gp(g):
                tok_w.append(DLG.issue(
                    g, out=wrtb.rearrange("p (kc e) -> p kc e", e=NEXP),
                    in_=Wrt.rearrange("(kc p) e -> p kc e", p=P)))
            block.gpsimd(r_gp)

            def r_mm(t):
                wtok(t, tok_w)
                for kc in range(KC):
                    for n in range(8):
                        r = t.matmul(PSA[0:NEXP, n * 512:(n + 1) * 512],
                                     wrtb[:, kc * NEXP:(kc + 1) * NEXP],
                                     ybf3[:, kc, n * 512:(n + 1) * 512],
                                     start=(kc == 0), stop=(kc == KC - 1))
                I(r, "MM")
            block.tensor(r_mm)
            mm1 = C["MM"]

            def r_ev(v, mm1=mm1):
                v.wait_ge(sems["MM"], mm1)
                r = v.tensor_copy(scT[:, :], PSA[0:NEXP, :])
                I(r, "EV")
            block.vector(r_ev)
            ev1_ = C["EV"]

            def r_tr(t, ev1_=ev1_):
                t.wait_ge(sems["EV"], ev1_)
                for ci in range(32):
                    r = t.transpose(PSA[:, ci * NEXP:(ci + 1) * NEXP],
                                    scT[:, ci * P:(ci + 1) * P],
                                    identf[0:NEXP, 0:NEXP])
                I(r, "MM")
            block.tensor(r_tr)
            mm2 = C["MM"]

            def r_soft(v, mm2=mm2):
                v.wait_ge(sems["MM"], mm2)
                r = v.tensor_copy(S[:, :], PSA[:, 0:32 * NEXP])
                I(r, "EV")
                S3 = S.rearrange("p (a b) -> p a b", b=NEXP)
                Sm3 = Sm.rearrange("p (a b) -> p a b", b=1)
                Se3 = Se.rearrange("p (a b) -> p a b", b=NEXP)
                mb = Sm3.broadcast_to([P, 32, NEXP])
                v.tensor_reduce(out=Sm3, in_=S3, axis=AX.X, op=ALU.max)
                v.tensor_copy(fbv, Sm[:, 31:32])
                v.tensor_tensor(out=Se3, in0=S3, in1=mb, op=ALU.subtract)
                r = v.tensor_copy(fbv, Se[:, 32 * NEXP - 1:32 * NEXP])
                I(r, "EV")
            block.vector(r_soft)
            evS = C["EV"]

            def r_exp(sc, evS=evS):
                sc.wait_ge(sems["EV"], evS)
                sc.activation(out=Se[:, :], in_=Se[:, :], func=AF.Exp)
                r = sc.activation(out=fba, in_=Se[:, 32 * NEXP - 1:32 * NEXP],
                                  func=AF.Copy)
                I(r, "AC")
            block.scalar(r_exp)
            acS = C["AC"]

            def r_top2(v, acS=acS):
                v.wait_ge(sems["AC"], acS)
                S3 = S.rearrange("p (a b) -> p a b", b=NEXP)
                Sm3 = Sm.rearrange("p (a b) -> p a b", b=1)
                Sr3 = Sr.rearrange("p (a b) -> p a b", b=1)
                Se3 = Se.rearrange("p (a b) -> p a b", b=NEXP)
                mb = Sm3.broadcast_to([P, 32, NEXP])
                mbr = Sr3.broadcast_to([P, 32, NEXP])
                v.tensor_reduce(out=Sm3, in_=Se3, axis=AX.X, op=ALU.add)
                v.tensor_copy(fbv, Sm[:, 31:32])
                v.reciprocal(Sr[:, :], Sm[:, :])
                v.tensor_copy(fbv, Sr[:, 31:32])
                v.tensor_tensor(out=Se3, in0=Se3, in1=mbr, op=ALU.mult)
                v.tensor_reduce(out=Sm3, in_=Se3, axis=AX.X, op=ALU.max)
                v.tensor_copy(fbv, Sm[:, 31:32])
                v.tensor_tensor(out=S3, in0=Se3, in1=mb, op=ALU.is_lt)
                v.tensor_tensor(out=S3, in0=Se3, in1=S3, op=ALU.mult)
                v.tensor_reduce(out=Sm3, in_=S3, axis=AX.X, op=ALU.max)
                v.tensor_copy(fbv, Sm[:, 31:32])
                v.tensor_tensor(out=S3, in0=Se3, in1=mb, op=ALU.is_ge)
                v.tensor_tensor(out=Se3, in0=Se3, in1=S3, op=ALU.mult)
                r = v.tensor_copy(fbv, Se[:, 32 * NEXP - 1:32 * NEXP])
                I(r, "EV")
            block.vector(r_top2)
            evT = C["EV"]

            def r_trb(t, evT=evT):
                t.wait_ge(sems["EV"], evT)
                for ci in range(32):
                    r = t.transpose(PSA[0:NEXP, ci * P:(ci + 1) * P],
                                    Se[:, ci * NEXP:(ci + 1) * NEXP], identf[:, :])
                I(r, "MM")
            block.tensor(r_trb)
            mm3 = C["MM"]

            def r_evb(v, mm3=mm3):
                v.wait_ge(sems["MM"], mm3)
                v.tensor_copy(ctT[:, :], PSA[0:NEXP, :])
                r = v.tensor_copy(fbv[0:NEXP, :], ctT[:, T - 1:T])
                I(r, "EV")
            block.vector(r_evb)
            evB = C["EV"]

            def r_st(sc, evB=evB):
                sc.wait_ge(sems["EV"], evB)
                DLA.issue(sc, out=cbT_d[:, :], in_=ctT[:, :])
            block.scalar(r_st)
            AR.pop()

        def emit_gateup(ybf3, inv):
            AR.push()
            gslab = [AR.alloc([P, KC * P], BF16) for _ in range(2)]
            uslab = [AR.alloc([P, KC * P], BF16) for _ in range(2)]
            silu_g = AR.alloc([P, T], F32)
            cb_sb = AR.alloc([P, T], F32)
            hstage = AR.alloc([P, T], BF16)
            wdc = AR.alloc([P, D], BF16) if inv == 0 else None

            gr, ur, hr, wdr = Ring(2), Ring(2), Ring(1), Ring(1)
            cb_prev_gp = [0]
            prev_hev = [None]
            wd_i = [0]

            exlist = [("S", MS)] + [(e, ME) for e in range(NEXP)]

            for ex, MX in exlist:
                tok_cb = []
                if ex != "S":
                    def cb_ld(s, ex=ex, w=cb_prev_gp[0], tok_cb=tok_cb):
                        if w:
                            s.wait_ge(sems["GP"], w)
                        src = cbT_d[ex:ex + 1, :].partition_broadcast(P).squeeze(1)
                        tok_cb.append(DLS.issue(s, out=cb_sb[:, :], in_=src))
                    block.sync(cb_ld)
                ex_last_gp = C["GP"]

                for m in range(MX):
                    tok_sl = []

                    def slabs(g, ex=ex, m=m, tok_sl=tok_sl):
                        if ex == "S":
                            gsrc = Wsg[:, m * P:(m + 1) * P]
                            usrc = Wsu[:, m * P:(m + 1) * P]
                        else:
                            gsrc = Wg[ex, :, m * P:(m + 1) * P]
                            usrc = Wu[ex, :, m * P:(m + 1) * P]
                        sg = gr.before_fill(g)
                        tok_sl.append(DLG.issue(
                            g, out=gslab[sg].rearrange("p (kc m) -> p kc m", m=P),
                            in_=gsrc.rearrange("(kc p) m -> p kc m", p=P)))
                        su = ur.before_fill(g)
                        tok_sl.append(DLG.issue(
                            g, out=uslab[su].rearrange("p (kc m) -> p kc m", m=P),
                            in_=usrc.rearrange("(kc p) m -> p kc m", p=P)))
                        if inv == 0 and wd_i[0] < NEK2:
                            i = wd_i[0]
                            if i < NEXP * ME:
                                e2, k2 = divmod(i, ME)
                                src = Wdn[e2, k2 * P:(k2 + 1) * P, :]
                            else:
                                src = Wsd[(i - NEXP * ME) * P:
                                          (i - NEXP * ME + 1) * P, :]
                            wdr.before_fill(g)
                            tok_sl.append(DLG.issue(g, out=wdc[:, :], in_=src))
                    block.gpsimd(slabs)
                    sl = (gr.fills - 1) % 2

                    if inv == 0 and wd_i[0] < NEK2:
                        i = wd_i[0]
                        wd_i[0] += 1

                        def wd_st(sc, i=i, tok_sl=tok_sl):
                            wtok(sc, tok_sl[-1:])
                            tk = DLA.issue(sc, out=wdT[i * P:(i + 1) * P, :],
                                           in_=wdc[:, :])
                            wdr.consumed(*tk)
                        block.scalar(wd_st)

                    mmg0 = C["MM"]

                    def gate_mm(t, tok_sl=tok_sl, sl=sl, w=prev_hev[0]):
                        wtok(t, tok_sl)
                        for k in range(KC):
                            for n in range(8):
                                if k == 0 and w is not None:
                                    t.wait_ge(sems["EV"], w - 9 + n + 1)
                                r = t.matmul(PSA[:, n * 512:(n + 1) * 512],
                                             gslab[sl][:, k * P:(k + 1) * P],
                                             ybf3[:, k, n * 512:(n + 1) * 512],
                                             start=(k == 0), stop=(k == KC - 1))
                                if k == KC - 1:
                                    I(r, "MM")
                    block.tensor(gate_mm)
                    mmg1 = C["MM"]
                    gr.consumed(sems["MM"], mmg1)

                    def gate_silu(sc, mmg0=mmg0):
                        for n in range(8):
                            sc.wait_ge(sems["MM"], mmg0 + n + 1)
                            r = sc.activation(out=silu_g[:, n * 512:(n + 1) * 512],
                                              in_=PSA[:, n * 512:(n + 1) * 512],
                                              func=AF.Silu)
                            I(r, "AC")
                        r = sc.activation(out=fba, in_=silu_g[:, T - 1:T],
                                          func=AF.Copy)
                        I(r, "AC")
                    block.scalar(gate_silu)
                    acg1 = C["AC"]

                    if ex != "S":
                        def gc(g, acg1=acg1, tok_cb=tok_cb):
                            g.wait_ge(sems["AC"], acg1)
                            wtok(g, tok_cb)
                            g.tensor_tensor(out=silu_g[:, :], in0=silu_g[:, :],
                                            in1=cb_sb[:, :], op=ALU.mult)
                            r = g.tensor_copy(fbg, silu_g[:, T - 1:T])
                            I(r, "GP")
                        block.gpsimd(gc)
                        ex_last_gp = C["GP"]
                    gp1 = C["GP"]
                    use_gc = (ex != "S")

                    mmu0 = C["MM"]

                    def up_mm(t, sl=sl, acg1=acg1):
                        for k in range(KC):
                            for n in range(8):
                                if k == 0:
                                    t.wait_ge(sems["AC"], acg1 - 9 + n + 1)
                                r = t.matmul(PSA[:, n * 512:(n + 1) * 512],
                                             uslab[sl][:, k * P:(k + 1) * P],
                                             ybf3[:, k, n * 512:(n + 1) * 512],
                                             start=(k == 0), stop=(k == KC - 1))
                                if k == KC - 1:
                                    I(r, "MM")
                    block.tensor(up_mm)
                    mmu1 = C["MM"]
                    ur.consumed(sems["MM"], mmu1)

                    def h_ev(v, mmu0=mmu0, gp1=gp1, acg1=acg1, use_gc=use_gc):
                        hr.before_fill(v)
                        if use_gc:
                            v.wait_ge(sems["GP"], gp1)
                        else:
                            v.wait_ge(sems["AC"], acg1)
                        for n in range(8):
                            v.wait_ge(sems["MM"], mmu0 + n + 1)
                            r = v.tensor_tensor(out=hstage[:, n * 512:(n + 1) * 512],
                                                in0=silu_g[:, n * 512:(n + 1) * 512],
                                                in1=PSA[:, n * 512:(n + 1) * 512],
                                                op=ALU.mult)
                            I(r, "EV")
                        r = v.tensor_copy(fbv, hstage[:, T - 1:T])
                        I(r, "EV")
                    block.vector(h_ev)
                    evh = C["EV"]
                    prev_hev[0] = evh

                    def h_st(sc, evh=evh, ex=ex, m=m):
                        sc.wait_ge(sems["EV"], evh)
                        row0 = (NEXP * DE + m * P) if ex == "S" else (ex * DE + m * P)
                        tk = DLA.issue(sc, out=h_s[row0:row0 + P, :],
                                       in_=hstage[:, :])
                        hr.consumed(*tk)
                    block.scalar(h_st)

                cb_prev_gp[0] = ex_last_gp
            AR.pop()

        def ek2row(i):
            if i < NEXP * ME:
                e2, k2 = divmod(i, ME)
                return e2 * DE + k2 * P
            return NEXP * DE + (i - NEXP * ME) * P

        def emit_down(xsrc, dst):
            AR.push()
            wdt = [AR.alloc([P, 2 * 512], BF16) for _ in range(4)]
            ht = [AR.alloc([P, 2 * TQ], BF16) for _ in range(4)]
            xt = [AR.alloc([P, TQ], F32) for _ in range(4)]
            o1 = [AR.alloc([P, TQ], F32) for _ in range(2)]
            wdr = Ring(4)
            hri = Ring(4)
            xri = Ring(4)
            o1r = Ring(2)
            prev_ev = [None, None, None, None]
            NPAIR = NEK2 // 2

            for ns in range(4):
                for m2g in range(4):
                    mmd0 = C["MM"]
                    for ii in range(NPAIR):
                        row = 2 * ii * P
                        tok = []

                        def dn_ld(s, row=row, ns=ns, m2g=m2g, tok=tok):
                            sw = wdr.before_fill(s)
                            tok.append(DLS.issue(
                                s,
                                out=wdt[sw].rearrange("p (a c) -> p a c", a=2),
                                in_=wdT[row:row + 2 * P,
                                        m2g * 512:(m2g + 1) * 512].rearrange(
                                            "(a p) c -> p a c", p=P)))
                            sh = hri.before_fill(s)
                            tok.append(DLS.issue(
                                s,
                                out=ht[sh].rearrange("p (a c) -> p a c", a=2),
                                in_=h_s[row:row + 2 * P,
                                        ns * TQ:(ns + 1) * TQ].rearrange(
                                            "(a p) c -> p a c", p=P)))
                        block.sync(dn_ld)

                        def dn_mm(t, ii=ii, tok=tok, w=tuple(prev_ev)):
                            wtok(t, tok)
                            sw = (wdr.fills - 1) % 4
                            sh = (hri.fills - 1) % 4
                            for a in range(2):
                                i = 2 * ii + a
                                for mm2 in range(4):
                                    if i == 0 and w[mm2] is not None:
                                        t.wait_ge(sems["EV"], w[mm2])
                                    for nsub in range(2):
                                        r = t.matmul(
                                            PSA[:, (mm2 * 2 + nsub) * 512:
                                                   (mm2 * 2 + nsub + 1) * 512],
                                            wdt[sw][:, a * 512 + mm2 * P:
                                                    a * 512 + (mm2 + 1) * P],
                                            ht[sh][:, a * TQ + nsub * 512:
                                                   a * TQ + (nsub + 1) * 512],
                                            start=(i == 0), stop=(i == NEK2 - 1))
                            I(r, "MM")
                        block.tensor(dn_mm)
                        wdr.consumed(sems["MM"], C["MM"])
                        hri.consumed(sems["MM"], C["MM"])
                    mmd1 = C["MM"]

                    for mm2 in range(4):
                        m2 = m2g * 4 + mm2
                        tokx = []

                        def x_ld(s, m2=m2, ns=ns, tokx=tokx):
                            sx = xri.before_fill(s)
                            tokx.append(DLS.issue(
                                s, out=xt[sx][:, :],
                                in_=xsrc[m2 * P:(m2 + 1) * P,
                                         ns * TQ:(ns + 1) * TQ]))
                        block.sync(x_ld)
                        sx = (xri.fills - 1) % 4

                        def dn_ev(v, mm2=mm2, tokx=tokx, sx=sx, mmd1=mmd1):
                            so = o1r.before_fill(v)
                            v.wait_ge(sems["MM"], mmd1)
                            wtok(v, tokx)
                            bank0 = (mm2 * 2) * 512
                            v.scalar_tensor_tensor(
                                out=o1[so][:, :], in0=PSA[:, bank0:bank0 + TQ],
                                scalar=0.5, in1=xt[sx][:, :],
                                op0=ALU.mult, op1=ALU.add)
                            r = v.tensor_copy(fbv, o1[so][:, TQ - 1:TQ])
                            I(r, "EV")
                        block.vector(dn_ev)
                        evd = C["EV"]
                        prev_ev[mm2] = evd
                        xri.consumed(sems["EV"], evd)

                        def o_st(sc, mm2=mm2, ns=ns, m2g=m2g, evd=evd):
                            sc.wait_ge(sems["EV"], evd)
                            m2 = m2g * 4 + mm2
                            so = (o1r.fills - 1) % 2
                            tk = DLA.issue(sc, out=dst[m2 * P:(m2 + 1) * P,
                                                       ns * TQ:(ns + 1) * TQ],
                                           in_=o1[so][:, :])
                            o1r.consumed(*tk)
                        block.scalar(o_st)
            AR.pop()

        AR.push()
        ybf = AR.alloc([P, KC * T], BF16).rearrange("p (kc t) -> p kc t", t=T)
        emit_ln_plain(xT, ybf)
        join_all()
        emit_router(ybf)
        join_all()
        emit_gateup(ybf, 0)
        if DEBUG_DUMP:
            join_all()

            def dmp(s):
                for kc in range(KC):
                    DLS.issue(s, out=dbg_y[kc * P:(kc + 1) * P, :],
                              in_=ybf[:, kc, :])
                for rr in range(0, HROWS, P):
                    DLS.issue(s, out=dbg_h[rr:rr + P, :],
                              in_=h_s[rr:rr + P, :])
                DLS.issue(s, out=dbg_cb[:, :], in_=cbT_d[:, :])
            block.sync(dmp)
        AR.pop()
        join_all()
        emit_down(xT, x1_s if stages > 1 else outT)
        join_all()

        try:
          if stages >= 2:
            AR.push()
            xnlast = AR.alloc([P, KC], F32)
            prevx = AR.alloc([P, KC], F32)
            finA = AR.alloc([P, KC], F32)
            finB = AR.alloc([P, KC], F32)

            AR.push()
            xf = [AR.alloc([P, TH], F32) for _ in range(3)]
            xbf = [AR.alloc([P, TH], BF16) for _ in range(2)]
            x2bf = [AR.alloc([P, TH], BF16) for _ in range(2)]
            A1 = [AR.alloc([P, TH], F32) for _ in range(2)]
            B1 = [AR.alloc([P, TH], F32) for _ in range(2)]
            A2 = [AR.alloc([P, TH], F32) for _ in range(2)]
            B2 = [AR.alloc([P, TH], F32) for _ in range(2)]
            mrep = AR.alloc([P, TH], F32)
            vrep = AR.alloc([P, TH], F32)
            t1 = AR.alloc([P, TH], F32)
            t2v = [AR.alloc([P, TH], F32) for _ in range(2)]
            xnt = [AR.alloc([P, TH], F32) for _ in range(2)]
            xr2 = Ring(3)
            cbr2 = Ring(2)
            sbr2 = Ring(2)
            xnr = Ring(2)
            t2r = Ring(2)
            rows2_prev = [None]

            for nh in range(2):
                for kc in range(KC):
                    tok = []

                    def ld(s, kc=kc, nh=nh, tok=tok):
                        slot = xr2.before_fill(s)
                        tok.append(DLS.issue(s, out=xf[slot][:, :],
                                             in_=x1_s[kc * P:(kc + 1) * P,
                                                      nh * TH:(nh + 1) * TH]))
                    block.sync(ld)
                    xslot = (xr2.fills - 1) % 3

                    def cast(v, tok=tok, xslot=xslot):
                        cbr2.before_fill(v)
                        wtok(v, tok)
                        r = v.tensor_copy(xbf[(cbr2.fills - 1) % 2][:, :],
                                          xf[xslot][:, :])
                        I(r, "EV")
                    block.vector(cast)
                    evc = C["EV"]
                    xr2.consumed(sems["EV"], evc)

                    def sq(sc, tok=tok, xslot=xslot):
                        sbr2.before_fill(sc)
                        wtok(sc, tok)
                        r = sc.activation(out=x2bf[(sbr2.fills - 1) % 2][:, :],
                                          in_=xf[xslot][:, :], func=AF.Square)
                        I(r, "AC")
                    block.scalar(sq)
                    acc_ = C["AC"]
                    xr2.consumed(sems["AC"], acc_)

                    def mm(t, kc=kc, evc=evc, acc_=acc_, wrows=rows2_prev[0]):
                        cslot = (cbr2.fills - 1) % 2
                        sslot = (sbr2.fills - 1) % 2
                        t.wait_ge(sems["EV"], evc)
                        t.wait_ge(sems["AC"], acc_)
                        if kc == 0 and wrows is not None:
                            t.wait_ge(sems["EV"], wrows)
                        st = (kc == 0)
                        sp = (kc == KC - 1)
                        for n in range(4):
                            r = t.matmul(PSA[:, n * 512:(n + 1) * 512], ones_bf[:, :],
                                         xbf[cslot][:, n * 512:(n + 1) * 512],
                                         start=st, stop=sp)
                        I(r, "MM")
                        for n in range(4):
                            r = t.matmul(PSA[:, 2048 + n * 512:2048 + (n + 1) * 512],
                                         ones_bf[:, :],
                                         x2bf[sslot][:, n * 512:(n + 1) * 512],
                                         start=st, stop=sp)
                        I(r, "MM")
                    block.tensor(mm)
                    cbr2.consumed(sems["MM"], C["MM"] - 1)
                    sbr2.consumed(sems["MM"], C["MM"])
                mmS1 = C["MM"]

                def rows(v, mmS1=mmS1):
                    v.wait_ge(sems["MM"], mmS1)
                    v.tensor_copy(mrep[:, :], PSA[:, 0:2048])
                    v.tensor_tensor(out=t1[:, :], in0=mrep[:, :], in1=mrep[:, :],
                                    op=ALU.mult)
                    v.tensor_tensor(out=vrep[:, :], in0=PSA[:, 2048:4096],
                                    in1=t1[:, :], op=ALU.subtract)
                    r = v.tensor_copy(fbv, vrep[:, TH - 1:TH])
                    I(r, "EV")
                block.vector(rows)
                evR = C["EV"]
                rows2_prev[0] = evR

                def rows_sc(sc, evR=evR, nh=nh):
                    sc.wait_ge(sems["EV"], evR)
                    sc.activation(out=t1[:, :], in_=vrep[:, :], func=AF.Ln,
                                  bias=epscol)
                    sc.activation(out=A1[nh][:, :], in_=t1[:, :], func=AF.Exp,
                                  scale=-0.5)
                    r = sc.activation(out=fba, in_=A1[nh][:, TH - 1:TH],
                                      func=AF.Copy)
                    I(r, "AC")
                block.scalar(rows_sc)
                acR = C["AC"]

                def rows_v2(v, acR=acR, nh=nh):
                    v.wait_ge(sems["AC"], acR)
                    v.scalar_tensor_tensor(out=B1[nh][:, :], in0=mrep[:, :],
                                           scalar=-1.0, in1=A1[nh][:, :],
                                           op0=ALU.mult, op1=ALU.mult)
                    v.tensor_tensor(out=t1[:, :], in0=A1[nh][:, :], in1=A1[nh][:, :],
                                    op=ALU.mult)
                    v.tensor_tensor(out=t1[:, :], in0=vrep[:, :], in1=t1[:, :],
                                    op=ALU.mult)
                    r = v.tensor_copy(fbv, t1[:, TH - 1:TH])
                    I(r, "EV")
                block.vector(rows_v2)
                evR2 = C["EV"]

                def rows_sc2(sc, evR2=evR2, nh=nh):
                    sc.wait_ge(sems["EV"], evR2)
                    sc.activation(out=t1[:, :], in_=t1[:, :], func=AF.Ln,
                                  bias=epscol)
                    sc.activation(out=vrep[:, :], in_=t1[:, :], func=AF.Exp,
                                  scale=-0.5)
                    r = sc.activation(out=fba, in_=vrep[:, TH - 1:TH],
                                      func=AF.Copy)
                    I(r, "AC")
                block.scalar(rows_sc2)
                acR2 = C["AC"]

                def rows_v3(v, acR2=acR2, nh=nh):
                    v.wait_ge(sems["AC"], acR2)
                    v.tensor_tensor(out=A2[nh][:, :], in0=A1[nh][:, :],
                                    in1=vrep[:, :], op=ALU.mult)
                    v.tensor_tensor(out=B2[nh][:, :], in0=B1[nh][:, :],
                                    in1=vrep[:, :], op=ALU.mult)
                    r = v.tensor_copy(fbv, B2[nh][:, TH - 1:TH])
                    I(r, "EV")
                block.vector(rows_v3)
                evAB = C["EV"]

                def ab_st(sc, nh=nh, evAB=evAB):
                    sc.wait_ge(sems["EV"], evAB)
                    DLA.issue(sc, out=ab_s[nh:nh + 1, :], in_=A1[nh][0:1, :])
                    DLA.issue(sc, out=ab_s[2 + nh:3 + nh, :], in_=B1[nh][0:1, :])
                block.scalar(ab_st)

            for nh in range(2):
                evRep = C["EV"]
                for kc in range(KC):
                    tok = []

                    def ld(s, kc=kc, nh=nh, tok=tok):
                        slot = xr2.before_fill(s)
                        tok.append(DLS.issue(s, out=xf[slot][:, :],
                                             in_=x1_s[kc * P:(kc + 1) * P,
                                                      nh * TH:(nh + 1) * TH]))
                    block.sync(ld)
                    xslot = (xr2.fills - 1) % 3

                    def ap_v(v, kc=kc, nh=nh, tok=tok, xslot=xslot, evRep=evRep):
                        sx = xnr.before_fill(v)
                        st2 = t2r.before_fill(v)
                        v.wait_ge(sems["EV"], evRep)
                        wtok(v, tok)
                        v.tensor_tensor(out=t2v[st2][:, :], in0=xf[xslot][:, :],
                                        in1=A2[nh][:, :], op=ALU.mult)
                        v.tensor_tensor(out=t2v[st2][:, :], in0=t2v[st2][:, :],
                                        in1=B2[nh][:, :], op=ALU.add)
                        v.tensor_scalar(out=xnt[sx][:, :], in0=t2v[st2][:, :],
                                        scalar1=gcol[:, kc:kc + 1],
                                        scalar2=bcol[:, kc:kc + 1],
                                        op0=ALU.mult, op1=ALU.add)
                        r = v.tensor_copy(fbv, xnt[sx][:, TH - 1:TH])
                        I(r, "EV")
                        if nh == 1:
                            r = v.tensor_copy(xnlast[:, kc:kc + 1],
                                              xnt[sx][:, TH - 1:TH])
                            I(r, "EV")
                    block.vector(ap_v)
                    evX = C["EV"]
                    xr2.consumed(sems["EV"], evX)
                    t2r.consumed(sems["EV"], evX)

                    def ap_st(sc, kc=kc, nh=nh, evX=evX):
                        sc.wait_ge(sems["EV"], evX)
                        sx = (xnr.fills - 1) % 2
                        tk = DLA.issue(sc, out=xn_s[kc * P:(kc + 1) * P,
                                                    nh * TH:(nh + 1) * TH],
                                       in_=xnt[sx][:, :])
                        xnr.consumed(*tk)
                    block.scalar(ap_st)
            AR.pop()
            join_all()

            if SUBSTAGE == "a":

                def dmp_a(s):
                    for kc in range(KC):
                        DLS.issue(s, out=outT[kc * P:(kc + 1) * P, :],
                                  in_=xn_s[kc * P:(kc + 1) * P, :])
                block.sync(dmp_a)

            if SUBSTAGE == "a":
                raise _SubstageDone()

            def cA_gp(g):
                tk = DLG.issue(g, out=ccAi[:, :], in_=xnlast[:, :])
                wtok(g, tk)
                r = g.collective_compute("AllGather", ALU.bypass,
                                         replica_groups=RG,
                                         ins=[ccAi[:, :].opt()],
                                         outs=[ccAo[:, :].opt()])
                r.then_inc(sems["CCA"], 1)
            block.gpsimd(cA_gp)

            tok_pg = []

            def cA_ld(s):
                s.wait_ge(sems["CCA"], 1)
                tok_pg.append(DLS.issue(s, out=prevx[:, :], in_=ccAo[0:P, :]))
            block.sync(cA_ld)

            def cA_v(v):
                wtok(v, tok_pg)
                r = v.tensor_scalar(out=prevx[:, :], in0=prevx[:, :],
                                    scalar1=flagc[:, 0:1], scalar2=None,
                                    op0=ALU.mult)
                I(r, "EV")
            block.vector(cA_v)
            evPrevx = C["EV"]
            if SUBSTAGE == "ca":

                def dmp_ca(s):
                    s.wait_ge(sems["EV"], evPrevx)
                    for kc in range(KC):
                        DLS.issue(s, out=outT[kc * P:(kc + 1) * P, :],
                                  in_=xn_s[kc * P:(kc + 1) * P, :])
                    DLS.issue(s, out=outT[0:P, T - KC:T], in_=prevx[:, :])
                block.sync(dmp_ca)

            if SUBSTAGE == "ca":
                raise _SubstageDone()

            AR.push()
            kxb = AR.alloc([P, KC * TQ], BF16).rearrange("p (kc t) -> p kc t", t=TQ)
            vxb = AR.alloc([P, KC * TQ], BF16).rearrange("p (kc t) -> p kc t", t=TQ)
            rxb = AR.alloc([P, KC * TQ], BF16).rearrange("p (kc t) -> p kc t", t=TQ)
            xne = [AR.alloc([P, TQ + 1], F32) for _ in range(2)]
            sdif = AR.alloc([P, TQ], F32)
            dbc = AR.alloc([P, TQ], F32)
            ekb = AR.alloc([P, TQ], F32)
            ekvb = AR.alloc([P, TQ], F32)
            Abuf = AR.alloc([P, TQ + 1], F32)
            Bbuf = AR.alloc([P, TQ + 1], F32)
            nst = [AR.alloc([P, TQ], F32) for _ in range(2)]
            dstb = [AR.alloc([P, TQ], F32) for _ in range(2)]
            rstb = [AR.alloc([P, TQ], F32) for _ in range(2)]
            trs = AR.alloc([P, TQ], F32)
            kslab = [AR.alloc([P, KC * P], BF16) for _ in range(2)]
            vslab = [AR.alloc([P, KC * P], BF16) for _ in range(2)]
            rslab = [AR.alloc([P, KC * P], BF16) for _ in range(2)]
            xner = Ring(2)
            nring, dring, rring = Ring(2), Ring(2), Ring(2)
            ksr, vsr, rsr = Ring(2), Ring(2), Ring(2)

            def pre2b(g):
                r = g.memset(sdif[:, :], 1.0)
                I(r, "GP")
            block.gpsimd(pre2b)

            ek_ac = [None]
            ekv_ev = [None]
            rs_ac = [None]

            for q in range(4):
                for kc in range(KC):
                    tok = []

                    def mx_ld(s, q=q, kc=kc, tok=tok):
                        slot = xner.before_fill(s)
                        if q == 0:
                            tok.append(DLS.issue(
                                s, out=xne[slot][:, 1:TQ + 1],
                                in_=xn_s[kc * P:(kc + 1) * P, 0:TQ]))
                            s.wait_ge(sems["EV"], evPrevx)
                            tok.append(DLS.issue(
                                s, out=xne[slot][:, 0:1],
                                in_=prevx[:, kc:kc + 1]))
                        else:
                            tok.append(DLS.issue(
                                s, out=xne[slot][:, :],
                                in_=xn_s[kc * P:(kc + 1) * P,
                                         q * TQ - 1:(q + 1) * TQ]))
                    block.sync(mx_ld)
                    slot = (xner.fills - 1) % 2

                    def mx_v(v, q=q, kc=kc, tok=tok, slot=slot):
                        wtok(v, tok)
                        v.tensor_tensor(out=sdif[:, :], in0=xne[slot][:, 1:TQ + 1],
                                        in1=xne[slot][:, 0:TQ], op=ALU.subtract)
                        v.scalar_tensor_tensor(out=kxb[:, kc, :], in0=sdif[:, :],
                                               scalar=mkcol[:, kc:kc + 1],
                                               in1=xne[slot][:, 0:TQ],
                                               op0=ALU.mult, op1=ALU.add)
                        v.scalar_tensor_tensor(out=vxb[:, kc, :], in0=sdif[:, :],
                                               scalar=mvcol[:, kc:kc + 1],
                                               in1=xne[slot][:, 0:TQ],
                                               op0=ALU.mult, op1=ALU.add)
                        v.scalar_tensor_tensor(out=rxb[:, kc, :], in0=sdif[:, :],
                                               scalar=mrcol[:, kc:kc + 1],
                                               in1=xne[slot][:, 0:TQ],
                                               op0=ALU.mult, op1=ALU.add)
                        if kc == KC - 1:
                            v.memset(sdif[:, :], 1.0)
                        r = v.tensor_copy(fbv, rxb[:, kc, TQ - 1:TQ])
                        I(r, "EV")
                    block.vector(mx_v)
                    xner.consumed(sems["EV"], C["EV"])
                mixgp1 = C["GP"]
                mixev1 = C["EV"]

                for m in range(KC):
                    tok_sl = []

                    def at_slab(g, m=m, tok_sl=tok_sl):
                        sk = ksr.before_fill(g)
                        tok_sl.append(DLG.issue(
                            g, out=kslab[sk].rearrange("p (kc m) -> p kc m", m=P),
                            in_=Wk[:, m * P:(m + 1) * P].rearrange(
                                "(kc p) m -> p kc m", p=P)))
                        sv = vsr.before_fill(g)
                        tok_sl.append(DLG.issue(
                            g, out=vslab[sv].rearrange("p (kc m) -> p kc m", m=P),
                            in_=Wv[:, m * P:(m + 1) * P].rearrange(
                                "(kc p) m -> p kc m", p=P)))
                        sr = rsr.before_fill(g)
                        tok_sl.append(DLG.issue(
                            g, out=rslab[sr].rearrange("p (kc m) -> p kc m", m=P),
                            in_=Wr[:, m * P:(m + 1) * P].rearrange(
                                "(kc p) m -> p kc m", p=P)))
                    block.gpsimd(at_slab)
                    sl = (ksr.fills - 1) % 2

                    def at_mm(t, tok_sl=tok_sl, sl=sl, mixgp1=mixgp1,
                              mixev1=mixev1, wek=ek_ac[0], wekv=ekv_ev[0],
                              wrs=rs_ac[0]):
                        wtok(t, tok_sl)
                        t.wait_ge(sems["GP"], mixgp1)
                        t.wait_ge(sems["EV"], mixev1)
                        if wek is not None:
                            t.wait_ge(sems["AC"], wek)
                        if wekv is not None:
                            t.wait_ge(sems["EV"], wekv)
                        for k in range(KC):
                            for n in range(2):
                                r = t.matmul(PSA[:, n * 512:(n + 1) * 512],
                                             kslab[sl][:, k * P:(k + 1) * P],
                                             kxb[:, k, n * 512:(n + 1) * 512],
                                             start=(k == 0), stop=(k == KC - 1))
                        I(r, "MM")
                        for k in range(KC):
                            for n in range(2):
                                r = t.matmul(PSA[:, 1024 + n * 512:1024 + (n + 1) * 512],
                                             vslab[sl][:, k * P:(k + 1) * P],
                                             vxb[:, k, n * 512:(n + 1) * 512],
                                             start=(k == 0), stop=(k == KC - 1))
                        I(r, "MM")
                        if wrs is not None:
                            t.wait_ge(sems["AC"], wrs)
                        for k in range(KC):
                            for n in range(2):
                                r = t.matmul(PSA[:, 2048 + n * 512:2048 + (n + 1) * 512],
                                             rslab[sl][:, k * P:(k + 1) * P],
                                             rxb[:, k, n * 512:(n + 1) * 512],
                                             start=(k == 0), stop=(k == KC - 1))
                        I(r, "MM")
                    block.tensor(at_mm)
                    mmk1 = C["MM"]
                    ksr.consumed(sems["MM"], mmk1)
                    vsr.consumed(sems["MM"], mmk1)
                    rsr.consumed(sems["MM"], mmk1)

                    def at_ek(sc, mmk1=mmk1):
                        sc.wait_ge(sems["MM"], mmk1 - 2)
                        sc.activation(out=ekb[:, :], in_=PSA[:, 0:TQ],
                                      func=AF.Exp)
                        r = sc.activation(out=fba, in_=ekb[:, TQ - 1:TQ],
                                          func=AF.Copy)
                        I(r, "AC")
                        sc.wait_ge(sems["MM"], mmk1)
                        sc.activation(out=trs[:, :], in_=PSA[:, 2048:2048 + TQ],
                                      func=AF.Exp, scale=-1.0)
                        r = sc.activation(out=fba, in_=trs[:, TQ - 1:TQ],
                                          func=AF.Copy)
                        I(r, "AC")
                    block.scalar(at_ek)
                    ek_ac[0] = C["AC"] - 1
                    acEK = C["AC"]

                    def at_scan(v, q=q, m=m, mmk1=mmk1, acEK=acEK):
                        v.wait_ge(sems["AC"], acEK - 1)
                        v.wait_ge(sems["MM"], mmk1 - 1)
                        r = v.tensor_tensor(out=ekvb[:, :], in0=ekb[:, :],
                                            in1=PSA[:, 1024:1024 + TQ], op=ALU.mult)
                        I(r, "EV")
                        v.tensor_scalar(out=dbc[:, :], in0=sdif[:, :],
                                        scalar1=dcol[:, m:m + 1], scalar2=None,
                                        op0=ALU.mult)
                        if q == 0:
                            v.memset(Abuf[:, 0:1], 0.0)
                            v.memset(Bbuf[:, 0:1], 0.0)
                        else:
                            v.tensor_copy(Abuf[:, 0:1], finA[:, m:m + 1])
                            v.tensor_copy(Bbuf[:, 0:1], finB[:, m:m + 1])
                        v.tensor_tensor_scan(out=Abuf[:, 1:TQ + 1], data0=dbc[:, :],
                                             data1=ekvb[:, :], initial=Abuf[:, 0:1],
                                             op0=ALU.mult, op1=ALU.add)
                        v.tensor_tensor_scan(out=Bbuf[:, 1:TQ + 1], data0=dbc[:, :],
                                             data1=ekb[:, :], initial=Bbuf[:, 0:1],
                                             op0=ALU.mult, op1=ALU.add)
                        v.tensor_copy(finA[:, m:m + 1], Abuf[:, TQ:TQ + 1])
                        v.tensor_copy(finB[:, m:m + 1], Bbuf[:, TQ:TQ + 1])
                        sn = nring.before_fill(v)
                        dring.before_fill(v)
                        v.scalar_tensor_tensor(out=nst[sn][:, :], in0=ekvb[:, :],
                                               scalar=eucol[:, m:m + 1],
                                               in1=Abuf[:, 0:TQ],
                                               op0=ALU.mult, op1=ALU.add)
                        v.scalar_tensor_tensor(out=dstb[sn][:, :], in0=ekb[:, :],
                                               scalar=eucol[:, m:m + 1],
                                               in1=Bbuf[:, 0:TQ],
                                               op0=ALU.mult, op1=ALU.add)
                        v.wait_ge(sems["AC"], acEK)
                        r = v.tensor_scalar(out=trs[:, :], in0=trs[:, :],
                                            scalar1=1.0, scalar2=None, op0=ALU.add)
                        I(r, "EV")
                    block.vector(at_scan)
                    ekv_ev[0] = C["EV"] - 1
                    evND = C["EV"]

                    def at_rs(sc, evND=evND):
                        sr2 = rring.before_fill(sc)
                        sc.wait_ge(sems["EV"], evND)
                        sc.activation(out=rstb[sr2][:, :], in_=trs[:, :], func=AF.Ln)
                        sc.activation(out=rstb[sr2][:, :], in_=rstb[sr2][:, :],
                                      func=AF.Exp, scale=-1.0)
                        r = sc.activation(out=fba, in_=rstb[sr2][:, TQ - 1:TQ],
                                          func=AF.Copy)
                        I(r, "AC")
                    block.scalar(at_rs)
                    rs_ac[0] = C["AC"]
                    acRS = C["AC"]

                    def at_st(sc, q=q, m=m, evND=evND, acRS=acRS):
                        sc.wait_ge(sems["EV"], evND)
                        sn = (nring.fills - 1) % 2
                        tk = DLA.issue(sc, out=nl_s[m * P:(m + 1) * P,
                                                    q * TQ:(q + 1) * TQ],
                                       in_=nst[sn][:, :])
                        nring.consumed(*tk)
                        tk = DLA.issue(sc, out=dl_s[m * P:(m + 1) * P,
                                                    q * TQ:(q + 1) * TQ],
                                       in_=dstb[sn][:, :])
                        dring.consumed(*tk)
                        sc.wait_ge(sems["AC"], acRS)
                        sr2 = (rring.fills - 1) % 2
                        tk = DLA.issue(sc, out=rs_s[m * P:(m + 1) * P,
                                                    q * TQ:(q + 1) * TQ],
                                       in_=rstb[sr2][:, :])
                        rring.consumed(*tk)
                    block.scalar(at_st)
            AR.pop()
            join_all()

            if SUBSTAGE == "b":

                def dmp_b(s):
                    for kc in range(KC):
                        DLS.issue(s, out=outT[kc * P:(kc + 1) * P, :],
                                  in_=nl_s[kc * P:(kc + 1) * P, :])
                block.sync(dmp_b)

            if SUBSTAGE == "b":
                raise _SubstageDone()

            def cB_gp(g):
                t1_ = DLG.issue(g, out=ccBi[:, 0:KC], in_=finA[:, :])
                t2_ = DLG.issue(g, out=ccBi[:, KC:2 * KC], in_=finB[:, :])
                wtok(g, [t1_, t2_])
                r = g.collective_compute("AllGather", ALU.bypass,
                                         replica_groups=RG,
                                         ins=[ccBi[:, :].opt()],
                                         outs=[ccBo[:, :].opt()])
                r.then_inc(sems["CCB"], 1)
            block.gpsimd(cB_gp)

            A0c, B0c = finA, finB
            tok_b0 = []

            def cB_ld(s):
                s.wait_ge(sems["CCB"], 1)
                tok_b0.append(DLS.issue(s, out=A0c[:, :], in_=ccBo[0:P, 0:KC]))
                tok_b0.append(DLS.issue(s, out=B0c[:, :], in_=ccBo[0:P, KC:2 * KC]))
            block.sync(cB_ld)

            def cB_v(v):
                wtok(v, tok_b0)
                v.tensor_scalar(out=A0c[:, :], in0=A0c[:, :],
                                scalar1=flagc[:, 0:1], scalar2=None, op0=ALU.mult)
                r = v.tensor_scalar(out=B0c[:, :], in0=B0c[:, :],
                                    scalar1=flagc[:, 0:1], scalar2=None, op0=ALU.mult)
                I(r, "EV")
            block.vector(cB_v)
            join_all()

            AR.push()
            ramp = AR.alloc([P, T], F32)
            dpow = [AR.alloc([P, TQ], F32) for _ in range(2)]
            rwst = [AR.alloc([P, TQ], BF16) for _ in range(2)]
            nld = [AR.alloc([P, TQ], F32) for _ in range(4)]
            dld = [AR.alloc([P, TQ], F32) for _ in range(4)]
            rld = [AR.alloc([P, TQ], F32) for _ in range(4)]
            rwr = Ring(2)
            nldr, dldr, rldr = Ring(4), Ring(4), Ring(4)

            def c_ramp(g):
                g.iota(ramp[:, :], [[1, T]], channel_multiplier=0,
                       allow_small_or_imprecise_dtypes=True)
                r = g.tensor_copy(fbg, ramp[:, T - 1:T])
                I(r, "GP")
            block.gpsimd(c_ramp)
            gpRamp = C["GP"]
            dp_evs = []
            items = [(q, m) for q in range(4) for m in range(KC)]
            NIT = len(items)
            st1 = {}

            for it in range(NIT + 1):
                if it < NIT:
                    q, m = items[it]
                    ip = it % 2
                    tok = []

                    def pc_ld(s, q=q, m=m, tok=tok):
                        sn = nldr.before_fill(s)
                        tok.append(DLS.issue(s, out=nld[sn][:, :],
                                             in_=nl_s[m * P:(m + 1) * P,
                                                      q * TQ:(q + 1) * TQ]))
                        dldr.before_fill(s)
                        tok.append(DLS.issue(s, out=dld[sn][:, :],
                                             in_=dl_s[m * P:(m + 1) * P,
                                                      q * TQ:(q + 1) * TQ]))
                        rldr.before_fill(s)
                        tok.append(DLS.issue(s, out=rld[sn][:, :],
                                             in_=rs_s[m * P:(m + 1) * P,
                                                      q * TQ:(q + 1) * TQ]))
                    block.sync(pc_ld)
                    sn = (nldr.fills - 1) % 4
                    w_dp = dp_evs[it - 2] if it >= 2 else None

                    def pc_dp(sc, q=q, m=m, ip=ip, gpRamp=gpRamp, w=w_dp):
                        sc.wait_ge(sems["GP"], gpRamp)
                        if w is not None:
                            sc.wait_ge(sems["EV"], w)
                        sc.activation(out=dpow[ip][:, :],
                                      in_=ramp[:, q * TQ:(q + 1) * TQ],
                                      func=AF.Exp, scale=wcol[:, m:m + 1])
                        r = sc.activation(out=fba, in_=dpow[ip][:, TQ - 1:TQ],
                                          func=AF.Copy)
                        I(r, "AC")
                    block.scalar(pc_dp)
                    acDP = C["AC"]

                    def pc_v(v, m=m, ip=ip, tok=tok, acDP=acDP, sn=sn):
                        wtok(v, tok)
                        v.wait_ge(sems["AC"], acDP)
                        v.scalar_tensor_tensor(out=nld[sn][:, :], in0=dpow[ip][:, :],
                                               scalar=A0c[:, m:m + 1],
                                               in1=nld[sn][:, :],
                                               op0=ALU.mult, op1=ALU.add)
                        r = v.scalar_tensor_tensor(out=dld[sn][:, :],
                                                   in0=dpow[ip][:, :],
                                                   scalar=B0c[:, m:m + 1],
                                                   in1=dld[sn][:, :],
                                                   op0=ALU.mult, op1=ALU.add)
                        I(r, "EV")
                    block.vector(pc_v)
                    dp_evs.append(C["EV"])
                    st1[it] = (sn, C["EV"])

                if it >= 1:
                    jt = it - 1
                    qj, mj = items[jt]
                    snj, evCj = st1.pop(jt)

                    def pc_sc(sc, evCj=evCj, snj=snj):
                        sc.wait_ge(sems["EV"], evCj)
                        sc.activation(out=dld[snj][:, :], in_=dld[snj][:, :],
                                      func=AF.Ln)
                        sc.activation(out=dld[snj][:, :], in_=dld[snj][:, :],
                                      func=AF.Exp, scale=-1.0)
                        r = sc.activation(out=fba, in_=dld[snj][:, TQ - 1:TQ],
                                          func=AF.Copy)
                        I(r, "AC")
                    block.scalar(pc_sc)
                    acRC = C["AC"]

                    def pc_v2(v, acRC=acRC, snj=snj):
                        sw = rwr.before_fill(v)
                        v.wait_ge(sems["AC"], acRC)
                        v.tensor_tensor(out=nld[snj][:, :], in0=nld[snj][:, :],
                                        in1=dld[snj][:, :], op=ALU.mult)
                        v.tensor_tensor(out=rwst[sw][:, :], in0=nld[snj][:, :],
                                        in1=rld[snj][:, :], op=ALU.mult)
                        r = v.tensor_copy(fbv, rld[snj][:, TQ - 1:TQ])
                        I(r, "EV")
                    block.vector(pc_v2)
                    evRW = C["EV"]
                    nldr.consumed(sems["EV"], evRW, fill=jt)
                    dldr.consumed(sems["EV"], evRW, fill=jt)
                    rldr.consumed(sems["EV"], evRW, fill=jt)

                    def pc_st(sc, qj=qj, mj=mj, evRW=evRW):
                        sc.wait_ge(sems["EV"], evRW)
                        sw = (rwr.fills - 1) % 2
                        tk = DLA.issue(sc, out=rw_s[mj * P:(mj + 1) * P,
                                                    qj * TQ:(qj + 1) * TQ],
                                       in_=rwst[sw][:, :])
                        rwr.consumed(*tk)
                    block.scalar(pc_st)
            AR.pop()
            AR.pop()
            join_all()

            if SUBSTAGE == "c":

                def dmp_c(g):
                    for kc in range(KC):
                        DLG.issue(g, out=outT[kc * P:(kc + 1) * P, :],
                                  in_=rw_s[kc * P:(kc + 1) * P, :])
                block.gpsimd(dmp_c)

            if SUBSTAGE == "c":
                raise _SubstageDone()

            AR.push()
            rwb = AR.alloc([P, KC * T], BF16).rearrange("p (kc t) -> p kc t", t=T)
            woslab = [AR.alloc([P, KC * P], BF16) for _ in range(2)]
            x1row = [AR.alloc([P, TH], F32) for _ in range(2)]
            x2st = [AR.alloc([P, TH], F32) for _ in range(2)]
            twE = AR.alloc([P, 512], F32)
            A1e = AR.alloc([P, TH], F32)
            B1e = AR.alloc([P, TH], F32)
            wor = Ring(2)
            x1rr, x2sr = Ring(2), Ring(2)
            abr = Ring(1)
            tok_rw = []

            def wo_ldrw(s):
                for kc in range(KC):
                    tok_rw.append(DLS.issue(s, out=rwb[:, kc, :],
                                            in_=rw_s[kc * P:(kc + 1) * P, :]))
            block.sync(wo_ldrw)
            wo_prev_ev = [None]

            for m3 in range(KC):
                tok_wo = []

                def wo_slab(g, m3=m3, tok_wo=tok_wo):
                    sw = wor.before_fill(g)
                    tok_wo.append(DLG.issue(
                        g, out=woslab[sw].rearrange("p (kc m) -> p kc m", m=P),
                        in_=Wo[:, m3 * P:(m3 + 1) * P].rearrange(
                            "(kc p) m -> p kc m", p=P)))
                block.gpsimd(wo_slab)

                mmw0 = C["MM"]

                def wo_mm(t, tok_wo=tok_wo, w=wo_prev_ev[0]):
                    wtok(t, tok_wo + tok_rw)
                    sw = (wor.fills - 1) % 2
                    for k3 in range(KC):
                        for n in range(8):
                            if k3 == 0 and w is not None:
                                t.wait_ge(sems["EV"], w - 10 + (n // 4) * 5 +
                                          (n % 4) + 1)
                            r = t.matmul(PSA[:, n * 512:(n + 1) * 512],
                                         woslab[sw][:, k3 * P:(k3 + 1) * P],
                                         rwb[:, k3, n * 512:(n + 1) * 512],
                                         start=(k3 == 0), stop=(k3 == KC - 1))
                            if k3 == KC - 1:
                                I(r, "MM")
                block.tensor(wo_mm)
                mmw1 = C["MM"]
                wor.consumed(sems["MM"], mmw1)

                for half in range(2):
                    tok_rows = []

                    def wo_rows(s, m3=m3, half=half, tok_rows=tok_rows):
                        sy = x1rr.before_fill(s)
                        tok_rows.append(DLS.issue(
                            s, out=x1row[sy][:, :],
                            in_=x1_s[m3 * P:(m3 + 1) * P,
                                     half * TH:(half + 1) * TH]))
                        abr.before_fill(s)
                        tok_rows.append(DLS.issue(
                            s, out=A1e[:, :],
                            in_=ab_s[half:half + 1, :]
                            .partition_broadcast(P).squeeze(1)))
                        tok_rows.append(DLS.issue(
                            s, out=B1e[:, :],
                            in_=ab_s[2 + half:3 + half, :]
                            .partition_broadcast(P).squeeze(1)))
                    block.sync(wo_rows)

                    def wo_ev(v, mmw0=mmw0, half=half, tok_rows=tok_rows):
                        sx2 = x2sr.before_fill(v)
                        wtok(v, tok_rows)
                        sy = (x1rr.fills - 1) % 2
                        for n4 in range(4):
                            n = half * 4 + n4
                            v.wait_ge(sems["MM"], mmw0 + n + 1)
                            slp = slice(n * 512, (n + 1) * 512)
                            sll = slice(n4 * 512, (n4 + 1) * 512)
                            v.tensor_tensor(out=x2st[sx2][:, sll], in0=PSA[:, slp],
                                            in1=x1row[sy][:, sll], op=ALU.add)
                            I(r_ := v.tensor_tensor(out=twE[:, :],
                                                    in0=x1row[sy][:, sll],
                                                    in1=A1e[:, sll],
                                                    op=ALU.mult), "EV")
                            v.tensor_tensor(out=twE[:, :], in0=twE[:, :],
                                            in1=B1e[:, sll], op=ALU.add)
                            v.tensor_tensor(out=x2st[sx2][:, sll],
                                            in0=x2st[sx2][:, sll],
                                            in1=twE[:, :], op=ALU.add)
                        r = v.tensor_copy(fbv, x2st[sx2][:, TH - 1:TH])
                        I(r, "EV")
                    block.vector(wo_ev)
                    x1rr.consumed(sems["EV"], C["EV"])
                    abr.consumed(sems["EV"], C["EV"])
                    evWo = C["EV"]

                    def wo_st(sc, m3=m3, half=half, evWo=evWo):
                        sc.wait_ge(sems["EV"], evWo)
                        sx2 = (x2sr.fills - 1) % 2
                        dst2 = x2_s if stages > 2 else outT
                        tk = DLA.issue(sc, out=dst2[m3 * P:(m3 + 1) * P,
                                                    half * TH:(half + 1) * TH],
                                       in_=x2st[sx2][:, :])
                        x2sr.consumed(*tk)
                    block.scalar(wo_st)
                wo_prev_ev[0] = C["EV"]
            AR.pop()
            join_all()

          if stages >= 3:
            AR.push()
            ybf3b = AR.alloc([P, KC * T], BF16).rearrange("p (kc t) -> p kc t", t=T)
            emit_ln_plain(x2_s, ybf3b)
            join_all()
            emit_router(ybf3b)
            join_all()
            emit_gateup(ybf3b, 1)
            AR.pop()
            join_all()
            emit_down(x2_s, outT)
            join_all()
        except _SubstageDone:
            join_all()

        def fin(sc):
            for s, v in DLA.totals() + DLS.totals() + DLG.totals():
                sc.wait_ge(s, v)
        block.scalar(fin)

    return nc


_NC_CACHE = {}


def kernel(L0_attn_x, L0_attn_ln_scale, L0_attn_ln_bias, L0_attn_mix_k,
           L0_attn_mix_v, L0_attn_mix_r, L0_attn_time_first, L0_attn_time_decay,
           L0_attn_Wk, L0_attn_Wv, L0_attn_Wr, L0_attn_Wo, L0_ffn_W_router,
           L0_ffn_We_gate, L0_ffn_We_up, L0_ffn_We_down, L0_ffn_Ws_gate,
           L0_ffn_Ws_up, L0_ffn_Ws_down):
    x = np.asarray(L0_attn_x, dtype=np.float32)
    B, S, _ = x.shape
    nco = 8

    key = (STAGES, SUBSTAGE, DEBUG_DUMP)
    if key not in _NC_CACHE:
        _NC_CACHE[key] = build_nc(STAGES)
    nc = _NC_CACHE[key]

    common = {
        "Wk": np.ascontiguousarray(L0_attn_Wk, dtype=np.float32),
        "Wv": np.ascontiguousarray(L0_attn_Wv, dtype=np.float32),
        "Wr": np.ascontiguousarray(L0_attn_Wr, dtype=np.float32),
        "Wo": np.ascontiguousarray(L0_attn_Wo, dtype=np.float32),
        "lnsc": np.ascontiguousarray(L0_attn_ln_scale, dtype=np.float32),
        "lnbs": np.ascontiguousarray(L0_attn_ln_bias, dtype=np.float32),
        "mixk": np.ascontiguousarray(L0_attn_mix_k, dtype=np.float32),
        "mixv": np.ascontiguousarray(L0_attn_mix_v, dtype=np.float32),
        "mixr": np.ascontiguousarray(L0_attn_mix_r, dtype=np.float32),
        "tfirst": np.ascontiguousarray(L0_attn_time_first, dtype=np.float32),
        "tdecay": np.ascontiguousarray(L0_attn_time_decay, dtype=np.float32),
        "Wrt": np.ascontiguousarray(L0_ffn_W_router, dtype=np.float32),
        "Wg": np.ascontiguousarray(L0_ffn_We_gate, dtype=np.float32),
        "Wu": np.ascontiguousarray(L0_ffn_We_up, dtype=np.float32),
        "Wdn": np.ascontiguousarray(L0_ffn_We_down, dtype=np.float32),
        "Wsg": np.ascontiguousarray(L0_ffn_Ws_gate, dtype=np.float32),
        "Wsu": np.ascontiguousarray(L0_ffn_Ws_up, dtype=np.float32),
        "Wsd": np.ascontiguousarray(L0_ffn_Ws_down, dtype=np.float32),
    }
    in_maps = []
    for c in range(nco):
        b, half = c // 2, c % 2
        xs = np.ascontiguousarray(x[b, half * T:(half + 1) * T, :].T)
        fl = np.full((P, 1), 1.0 if half == 1 else 0.0, dtype=np.float32)
        m = dict(common)
        m["xT"] = xs
        m["flag"] = fl
        in_maps.append(m)

    res = run_bass_kernel_spmd(nc, in_maps, core_ids=list(range(nco)))

    out = np.empty((B, S, D), dtype=np.float32)
    for c in range(nco):
        b, half = c // 2, c % 2
        out[b, half * T:(half + 1) * T, :] = res.results[c]["outT"].T
    return out
